# version 12
# baseline (speedup 1.0000x reference)
import math
import numpy as np
import concourse.bass as bass
import concourse.mybir as mybir
from concourse.bass_utils import run_bass_kernel_spmd

F32 = mybir.dt.float32
BF16 = mybir.dt.bfloat16
AF = mybir.ActivationFunctionType
ALU = mybir.AluOpType
AX = mybir.AxisListType

NCORES = 8
SEQS = 2
T = 2048
NTT = 16
D = 1024
DC = 8
H = 16
DR = 32
KVR = 256
QR = 512
FF = 2816
FFC = 22
NG = 4
GH = 4
SCALE = 1.0 / math.sqrt(96.0)
CSHIFT = -12.0
MH = 6291456.0
M1 = 12582912.0
M2 = 25165824.0
SLOT = [2, 0, 3, 1]
DEBUG = False
PUMP_EAGER = True

ENGS = ("sync", "gpsimd", "tensor", "vector", "scalar")

_NAME_N = [0]


def _nm(pfx="t"):
    _NAME_N[0] += 1
    return f"{pfx}{_NAME_N[0]}"



class _Sem:
    __slots__ = ("h", "v")

    def __init__(self, h):
        self.h = h
        self.v = 0


class Prog:
    def __init__(self, nc, stack):
        self.nc = nc
        self.ops = {e: [] for e in ENGS}
        self.last_inst = {e: None for e in ENGS}
        self.TEN = _Sem(stack.enter_context(nc.semaphore("sTEN")))
        self.VEC = _Sem(stack.enter_context(nc.semaphore("sVEC")))
        self.SCA = _Sem(stack.enter_context(nc.semaphore("sSCA")))
        self.GPS = _Sem(stack.enter_context(nc.semaphore("sGPS")))
        self.FEN = _Sem(stack.enter_context(nc.semaphore("sFEN")))
        self._stack = stack
        self.dma_sems = []
        self._fence_marks = {}
        self.esem = {"tensor": self.TEN, "vector": self.VEC,
                     "scalar": self.SCA, "gpsimd": self.GPS}

    def dsem(self, name):
        s = _Sem(self._stack.enter_context(self.nc.semaphore(name)))
        self.dma_sems.append(s)
        return s

    def add(self, eng, emit, waits=(), sig=None, dma=None):
        w = [(s.h, v) for (s, v) in waits if s is not None]
        snap = None
        incs = []
        if sig is not None:
            sig.v += 1
            incs.append((sig.h, 1))
            snap = (sig, sig.v)
        if dma is not None:
            dma.v += 16
            incs.append((dma.h, 16))
            snap = (dma, dma.v)
        self.ops[eng].append((w, emit, incs, dma is not None))
        return snap

    def dma(self, eng, out_ap, in_ap, sem, waits=(), transpose=False):

        def emit(e, o=out_ap, i=in_ap, tp=transpose):
            if tp:
                return e.dma_start_transpose(o, i)
            return e.dma_start(o, i)

        return self.add(eng, emit, waits=waits, dma=sem)

    def _ensure_signaled(self, e):
        sem = self.esem[e]
        for op in reversed(self.ops[e]):
            w, emit, incs, isdma = op
            if emit is None or isdma:
                continue
            for (h, amt) in incs:
                if h is sem.h:
                    return
            sem.v += 1
            incs.append((sem.h, 1))
            return

    def vsep(self):
        self.ops["vector"].append(([], lambda e: e.drain(), [], False))

    def fence(self):
        for e in ("tensor", "vector", "scalar", "gpsimd"):
            self._ensure_signaled(e)
        waits = [(self.TEN.h, self.TEN.v), (self.VEC.h, self.VEC.v),
                 (self.SCA.h, self.SCA.v), (self.GPS.h, self.GPS.v)]
        for s in self.dma_sems:
            if self._fence_marks.get(id(s)) != s.v:
                waits.append((s.h, s.v))
                self._fence_marks[id(s)] = s.v
        for e in ENGS:
            self.ops[e].append((list(waits), None, [], False))

    def flush(self):
        with self.nc.Block() as block:
            for engname in ENGS:
                lst = self.ops[engname]
                if not lst:
                    continue

                def body(eng, mylst=lst, en=engname):
                    for (w, emit, incs, isdma) in mylst:
                        for (h, v) in w:
                            eng.wait_ge(h, v)
                        inst = emit(eng) if emit is not None else None
                        if inst is not None:
                            self.last_inst[en] = inst
                        if incs:
                            tgt = inst if inst is not None else self.last_inst[en]
                            if tgt is None:
                                for (h, amt) in incs:
                                    eng.sem_inc(h, amt)
                            else:
                                for (h, amt) in incs:
                                    tgt.then_inc(h, amt)

                getattr(block, engname)(body)
        self.ops = {e: [] for e in ENGS}


class QBufs:

    def __init__(self, stack, nc, qw, side="left"):
        sb = lambda shape, d: stack.enter_context(
            nc.sbuf_tensor(_nm(), shape, d, side=side))
        self.qw = qw
        nb = qw // 16
        self.Q1 = sb([128, qw], F32)
        self.Q2 = sb([128, qw], F32)
        self.Q3 = sb([128, qw], F32)
        self.Qs = sb([128, qw], BF16)
        self.sm = sb([128, 3 * nb], F32)
        self.nb = nb


def quant_unit(p, qb, in_ap, nb, out_ap, pre_waits=()):
    n = nb * 16
    NB = qb.nb
    bm = qb.sm[:, 0:nb]
    sS = qb.sm[:, NB:NB + nb]
    sinv = qb.sm[:, 2 * NB:2 * NB + nb]
    in3 = in_ap.rearrange("p (b k) -> p b k", k=16)
    p.add("vector", lambda e, i=in3, o=bm.rearrange("p (b o) -> p b o", o=1):
          e.tensor_reduce(o, i, AX.X, ALU.max, apply_absolute_value=True),
          waits=pre_waits)
    p.vsep()
    p.add("vector", lambda e, i=bm: e.tensor_single_scalar(i, i, 6e-12, ALU.max))
    p.vsep()
    p.add("vector", lambda e, i=bm, o=sS:
          e.tensor_single_scalar(o, i, 1.0 / 6.0, ALU.mult))
    p.vsep()
    p.add("vector", lambda e, i=sS, o=sinv: e.reciprocal(o, i))
    p.vsep()
    sinv_b = bass.AP(qb.sm, 2 * NB, [[3 * NB, 128], [1, nb], [0, 16]])
    s_b = bass.AP(qb.sm, NB, [[3 * NB, 128], [1, nb], [0, 16]])
    z = qb.Q1[:, 0:n]
    a = qb.Q2[:, 0:n]
    zw = p.add("vector", lambda e, i=in3, o=z.rearrange("p (b k) -> p b k", k=16),
               sv=sinv_b: e.tensor_tensor(o, i, sv, ALU.mult), sig=p.VEC)
    aw = p.add("scalar", lambda e, i=z, o=a: e.activation(o, i, AF.Abs),
               waits=[zw], sig=p.SCA)
    gw = p.add("scalar", lambda e, i=z, o=qb.Qs[:, 0:n]:
               e.activation(o, i, AF.Sign), sig=p.SCA)
    p.add("vector", lambda e, i=a, o=z:
          e.tensor_scalar(o, i, MH, 2.0 + MH, ALU.add, ALU.min), waits=[gw])
    p.add("vector", lambda e, i=a, o=qb.Q3[:, 0:n]:
          e.tensor_scalar(o, i, M1, 2.0 + M1, ALU.add, ALU.max))
    p.add("vector", lambda e, i=qb.Q3[:, 0:n]:
          e.tensor_scalar(i, i, 4.0 + M1, M1 + 2.0, ALU.min, ALU.subtract))
    p.add("vector", lambda e, i=a:
          e.tensor_scalar(i, i, M2, 4.0 + M2, ALU.add, ALU.max))
    p.add("vector", lambda e, i=a:
          e.tensor_scalar(i, i, 6.0 + M2, M2 + 4.0, ALU.min, ALU.subtract))
    p.add("vector", lambda e, i=z, j=qb.Q3[:, 0:n]:
          e.scalar_tensor_tensor(i, i, MH, j, ALU.subtract, ALU.add))
    p.add("vector", lambda e, i=z, j=a:
          e.tensor_tensor(i, i, j, ALU.add))
    p.add("vector", lambda e, i=z, j=qb.Qs[:, 0:n]:
          e.tensor_tensor(i, i, j, ALU.mult))
    w = p.add("vector",
              lambda e, i=z.rearrange("p (b k) -> p b k", k=16),
              o=out_ap.rearrange("p (b k) -> p b k", k=16), sv=s_b:
              e.tensor_tensor(o, i, sv, ALU.mult), sig=p.VEC)
    return w


def ln_unit(p, src_ap, stb, out_ap, junk, cst, ld_wait, extra_write_waits=()):
    sqw = p.add("scalar", lambda e, i=src_ap, o=junk[:, 0:D], ac=stb[:, 0:1]:
                e.activation(o, i, AF.Square, accum_out=ac),
                waits=[ld_wait], sig=p.SCA)
    p.add("vector", lambda e, i=src_ap, o=stb[:, 1:2]:
          e.reduce_sum(o, i, AX.X), waits=[ld_wait])
    p.vsep()
    p.add("vector", lambda e, i=stb[:, 1:2], o=stb[:, 2:3]:
          e.tensor_single_scalar(o, i, 1.0 / D, ALU.mult))
    p.vsep()
    p.add("vector", lambda e, i=stb[:, 2:3], o=stb[:, 3:4]:
          e.tensor_tensor(o, i, i, ALU.mult))
    p.vsep()
    vw = p.add("vector", lambda e, i=stb[:, 0:1], j=stb[:, 3:4], o=stb[:, 4:5]:
               e.scalar_tensor_tensor(o, i, 1.0 / D, j, ALU.mult, ALU.subtract),
               waits=[sqw], sig=p.VEC)
    sw = p.add("scalar", lambda e, i=stb[:, 4:5], o=stb[:, 5:6], b=cst[:, 1:2]:
               e.activation(o, i, AF.Sqrt, bias=b), waits=[vw], sig=p.SCA)
    p.add("vector", lambda e, i=stb[:, 5:6], o=stb[:, 6:7]:
          e.reciprocal(o, i), waits=[sw])
    p.vsep()
    xw = p.add("vector", lambda e, i=src_ap, o=out_ap, m=stb[:, 2:3],
               rs=stb[:, 6:7]:
               e.tensor_scalar(o, i, m, rs, ALU.subtract, ALU.mult),
               waits=list(extra_write_waits), sig=p.VEC)
    return xw, sqw


def build_program():
    from contextlib import ExitStack
    nc = bass.Bass("TRN2", target_bir_lowering=False)
    st = ExitStack()

    dt = nc.dram_tensor
    xh = dt("x", [SEQS, T, D], F32, kind="ExternalInput")
    yh = dt("y", [SEQS, T, D], F32, kind="ExternalOutput")
    w_kv_h = dt("w_kv", [128, DC, KVR], F32, kind="ExternalInput")
    w_q_h = dt("w_q", [128, DC, QR], F32, kind="ExternalInput")
    w_kr_h = dt("w_kr", [128, DC, QR], F32, kind="ExternalInput")
    w_qr_h = dt("w_qr", [128, 4, QR], F32, kind="ExternalInput")
    w_kd_h = dt("w_kd", [128, 2, D], F32, kind="ExternalInput")
    w_qd_h = dt("w_qd", [128, 4, D], F32, kind="ExternalInput")
    w_vd_h = dt("w_vd", [128, 2, D], F32, kind="ExternalInput")
    w_o_h = dt("w_o", [128, DC, D], F32, kind="ExternalInput")
    wg_h = dt("wg", [128, FFC, D], F32, kind="ExternalInput")
    wu_h = dt("wu", [128, FFC, D], F32, kind="ExternalInput")
    wd_h = dt("wd", [128, DC, FF], F32, kind="ExternalInput")
    cos_h = dt("cos_t", [128, NTT, DR], F32, kind="ExternalInput")
    sinp_h = dt("sinp_t", [128, NTT, DR], F32, kind="ExternalInput")
    tri_h = dt("tri", [128, 128], F32, kind="ExternalInput")
    g_h = dt("g_bc", [128, KVR], F32, kind="ExternalInput")
    cst_h = dt("cst", [128, 8], F32, kind="ExternalInput")

    wgq_s = dt("wgq_s", [D, FF], BF16, kind="Internal")
    wuq_s = dt("wuq_s", [D, FF], BF16, kind="Internal")
    wdq_s = dt("wdq_s", [FF, D], BF16, kind="Internal")
    x2_s = dt("x2_s", [SEQS, T, D], F32, kind="Internal")
    r_s = dt("r_s", [SEQS, H, T], BF16, kind="Internal")
    r_fs = dt("r_fs", [128, 4 * T], BF16, kind="Internal")
    r_s2 = dt("r_s2", [SEQS, H, T], F32, kind="Internal")

    p = Prog(nc, st)
    sb = lambda shape, d: st.enter_context(nc.sbuf_tensor(_nm(), shape, d))

    def tap(name, src_ap, shape, dtyp):
        if not DEBUG:
            return
        th = nc.dram_tensor("tap_" + name, shape, dtyp, kind="ExternalOutput")
        p.fence()
        p.dma("sync", th[:].rearrange("a b -> (a b)") if len(shape) == 2 else th[:],
              src_ap, p.dsem("TAP" + name))
        p.fence()

    tri = sb([128, 128], BF16)
    g_bc = sb([128, KVR], F32)
    cst = sb([128, 8], F32)
    junk = sb([128, D], BF16)
    stt_ = [sb([128, 8], F32) for _ in range(2)]
    psA = st.enter_context(nc.psum_tensor(_nm("psA"), [128, 1024], F32))
    psB = st.enter_context(nc.psum_tensor(_nm("psB"), [128, 1024], F32))
    ps = [st.enter_context(nc.psum_tensor(_nm("ps"), [128, 512], F32)) for _ in range(4)]
    ps = ps + [psA[:, 0:512], psA[:, 512:1024], psB[:, 0:512], psB[:, 512:1024]]
    psS = [psA, psB]

    CONG = p.dsem("CONG")
    CON = p.dsem("CON")
    CONSTS_G = p.dma("gpsimd", tri[:], tri_h[:], CONG)
    p.dma("sync", g_bc[:], g_h[:], CON)
    CONSTS = p.dma("sync", cst[:], cst_h[:], CON)


    sems_cache = {}
    for s in range(SEQS):
        with ExitStack() as seqst:
            prost = ExitStack()
            prep = None
            if s == 0:
                psb = lambda shape, d: prost.enter_context(
                    nc.sbuf_tensor(_nm(), shape, d, side="right"))
                wtin = [psb([128, 256], F32) for _ in range(4)]
                wtq = psb([128, 1408], BF16)
                wtT = psb([128, 2048], BF16)
                qbP = QBufs(prost, nc, 256, side="right")
                WIN = [p.dsem("WIN%d" % i) for i in range(4)]
                TPS = p.dsem("TPS")
                STS = p.dsem("STS")

                def prep_gen():
                    jobs = (
                        (wg_h, FFC, D, 2,
                         lambda nt, hf: bass.AP(wtT, (nt % 2) * 128,
                                                [[2048, 128], [256, DC], [1, 128]]),
                         lambda t0, w_, hf: (bass.AP(wtT, 0,
                                                     [[2048, 128], [256, DC],
                                                      [1, 128 * w_]]),
                                             bass.AP(wgq_s, t0 * 128,
                                                     [[FF, 128], [128 * FF, DC],
                                                      [1, 128 * w_]]))),
                        (wu_h, FFC, D, 2,
                         lambda nt, hf: bass.AP(wtT, (nt % 2) * 128,
                                                [[2048, 128], [256, DC], [1, 128]]),
                         lambda t0, w_, hf: (bass.AP(wtT, 0,
                                                     [[2048, 128], [256, DC],
                                                      [1, 128 * w_]]),
                                             bass.AP(wuq_s, t0 * 128,
                                                     [[FF, 128], [128 * FF, DC],
                                                      [1, 128 * w_]]))),
                        (wd_h, DC, FF, 1,
                         lambda nt, hf: bass.AP(wtT, 0,
                                                [[2048, 128], [128, 11], [1, 128]]),
                         lambda t0, w_, hf: (bass.AP(wtT, 0,
                                                     [[2048, 128], [128, 11],
                                                      [1, 128]]),
                                             bass.AP(wdq_s,
                                                     hf * 11 * 128 * D + t0 * 128,
                                                     [[D, 128], [128 * D, 11],
                                                      [1, 128]]))),
                    )
                    meta = []
                    for (srch, ntiles, K, WW, dstfn, storefn) in jobs:
                        NH = K // 1408 if K % 1408 == 0 else 1
                        HK = K // NH
                        for nt in range(ntiles):
                            for hf in range(NH):
                                q0 = 0
                                while q0 < HK:
                                    hw = min(256, HK - q0)
                                    meta.append(("q", srch, nt, hf, HK, q0, hw))
                                    q0 += hw
                                meta.append(("t", nt, hf, NH, HK, dstfn,
                                             storefn))
                            if NH == 1 and (nt % WW == WW - 1
                                            or nt == ntiles - 1):
                                t0 = (nt // WW) * WW
                                meta.append(("s", t0, nt - t0 + 1, storefn))
                    qlist = [m for m in meta if m[0] == "q"]
                    RING = len(wtin)
                    guard_in = [None] * RING
                    ld_tok = {}

                    def issue_load(i):
                        _, srch, nt, hf, HK, q0, hw = qlist[i]
                        r = i % RING
                        w = [CONSTS_G]
                        if guard_in[r] is not None:
                            w.append(guard_in[r])
                        ld_tok[i] = p.dma("gpsimd", wtin[r][:, 0:hw],
                                          srch[:, nt,
                                               hf * HK + q0:
                                               hf * HK + q0 + hw],
                                          WIN[r], waits=w)
                    for i in range(min(RING - 1, len(qlist))):
                        issue_load(i)
                    qi = 0
                    qw = None
                    tps_snap = None
                    sts_snap = None
                    tps_last = None
                    for m in meta:
                        if m[0] == "q":
                            _, srch, nt, hf, HK, q0, hw = m
                            if qi + RING - 1 < len(qlist):
                                issue_load(qi + RING - 1)
                            r = qi % RING
                            pw = [ld_tok.pop(qi)]
                            if q0 == 0 and tps_snap is not None:
                                pw.append(tps_snap)
                            qw = quant_unit(p, qbP, wtin[r][:, 0:hw],
                                            hw // 16,
                                            wtq[:, q0:q0 + hw],
                                            pre_waits=pw)
                            guard_in[r] = qw
                            qi += 1
                            yield
                        elif m[0] == "t":
                            _, nt, hf, NH, HK, dstfn, storefn = m
                            w = [qw]
                            if sts_snap is not None:
                                w.append(sts_snap)
                            tps_snap = p.dma("sync", dstfn(nt, hf),
                                             wtq[:, 0:HK], TPS, waits=w,
                                             transpose=True)
                            tps_last = tps_snap
                            if NH > 1:
                                sap, dap = storefn(nt, 1, hf)
                                sts_snap = p.dma("sync", dap, sap, STS,
                                                 waits=[tps_snap])
                            yield
                        else:
                            _, t0, w_, storefn = m
                            sap, dap = storefn(t0, w_, 0)
                            sts_snap = p.dma("sync", dap, sap, STS,
                                             waits=[tps_last])
                            yield
                prep = prep_gen()
                if PUMP_EAGER:
                    for _ in prep:
                        pass
                    p.fence()

            def pump(n=1):
                if prep is not None:
                    for _ in range(n):
                        next(prep, None)
            with ExitStack() as bst:
                bsb = lambda shape, d: bst.enter_context(
                    nc.sbuf_tensor(_nm(), shape, d, side="right"))
                kvn_fm = bsb([128, 2 * T], BF16)
                qc_fm = bsb([128, 4 * T], BF16)
                kr_tok = bsb([128, NTT * 512], BF16)
                qr_tok = bsb([128, NTT * 512], BF16)

                with ExitStack() as cst1:
                    csb = lambda shape, d: cst1.enter_context(
                        nc.sbuf_tensor(_nm(), shape, d, side="right"))
                    xn_fm = csb([128, DC * T], BF16)
                    w_kv = csb([128, DC * KVR], BF16)
                    w_q = csb([128, DC * QR], BF16)
                    w_kr = csb([128, DC * QR], BF16)
                    w_qr = csb([128, 4 * QR], BF16)
                    cos_t = csb([128, NTT * DR], F32)
                    sinp_t = csb([128, NTT * DR], F32)
                    if s == 0:
                        ROPE_T = p.dsem("ROPET")
                        LDX = [p.dsem("LDX0"), p.dsem("LDX1")]
                        TPN = [p.dsem("TPN0"), p.dsem("TPN1")]
                        KVT = [p.dsem("KVT0"), p.dsem("KVT1")]
                        RPT = [p.dsem("RPT0"), p.dsem("RPT1")]
                        WEARLY = p.dsem("WEARLY")
                        WLATE = p.dsem("WLATE")
                        sems_cache[0] = (ROPE_T, LDX, TPN, KVT, RPT, WEARLY, WLATE)
                    (ROPE_T, LDX, TPN, KVT, RPT, WEARLY, WLATE) = sems_cache[0]
                    p.dma("sync", cos_t[:],
                          cos_h[:].rearrange("p a b -> p (a b)"), ROPE_T)
                    rtw = p.dma("sync", sinp_t[:],
                                sinp_h[:].rearrange("p a b -> p (a b)"), ROPE_T)
                    xt = [csb([128, D], F32) for _ in range(2)]
                    xnt = [csb([128, D], BF16) for _ in range(2)]
                    kvt = [csb([128, KVR], BF16) for _ in range(2)]
                    rpt = [csb([128, QR], BF16) for _ in range(2)]
                    rt1 = csb([128, QR], F32)
                    rt2 = csb([128, QR], F32)

                    ew = []
                    for (dsth, srch) in ((w_kv, w_kv_h), (w_q, w_q_h),
                                         (w_kr, w_kr_h), (w_qr, w_qr_h)):
                        ew.append(p.dma("gpsimd", dsth[:],
                                        srch[:].rearrange("p a b -> p (a b)"),
                                        WEARLY, waits=[CONSTS_G]))
                    EARLYW = ew[-1]

                    xt_guard = [None, None]
                    xnt_guard = [None, None]
                    tp_last = None
                    for t in range(NTT):
                        r = t % 2
                        w = [CONSTS] if t < 2 else [xt_guard[r]]
                        ldx = p.dma("sync", xt[r][:],
                                    xh[s, t * 128:(t + 1) * 128, :], LDX[r],
                                    waits=w)
                        eww = [] if xnt_guard[r] is None else [xnt_guard[r]]
                        xw, _ = ln_unit(p, xt[r][:], stt_[r], xnt[r][:],
                                        junk, cst, ldx, extra_write_waits=eww)
                        xt_guard[r] = xw
                        tp_last = p.dma(
                            "sync",
                            bass.AP(xn_fm, t * 128,
                                    [[DC * T, 128], [T, DC], [1, 128]]),
                            xnt[r][:], TPN[t % 2], waits=[xw], transpose=True)
                        xnt_guard[r] = tp_last
                        pump(1)
                    XN_DONE = (TPN[0], TPN[0].v)
                    XN_DONE2 = (TPN[1], TPN[1].v)

                    kv_guard = [None, None]
                    tp_last = None
                    for t in range(NTT):
                        r = t % 2
                        stb = stt_[r]
                        for c in range(DC):
                            w = [XN_DONE, XN_DONE2, EARLYW] if (c == 0 and t == 0) else []
                            p.add("tensor", lambda e, a=bass.AP(
                                xn_fm, c * T + t * 128, [[DC * T, 128], [1, 128]]),
                                b=bass.AP(w_kv, c * KVR, [[DC * KVR, 128], [1, KVR]]),
                                o=ps[0][:, 0:KVR], s_=(c == 0), sp=(c == DC - 1):
                                e.matmul(o, a, b, start=s_, stop=sp),
                                waits=w, sig=p.TEN if c == DC - 1 else None)
                        mmw = (p.TEN, p.TEN.v)
                        sqw = p.add("scalar", lambda e, i=ps[0][:, 0:KVR],
                                    o=junk[:, 0:KVR], ac=stb[:, 0:1]:
                                    e.activation(o, i, AF.Square, accum_out=ac),
                                    waits=[mmw], sig=p.SCA)
                        rmw = p.add("scalar", lambda e, i=stb[:, 0:1], o=stb[:, 1:2],
                                    b=cst[:, 2:3]:
                                    e.activation(o, i, AF.Sqrt, bias=b,
                                                 scale=1.0 / KVR),
                                    waits=[sqw], sig=p.SCA)
                        p.add("vector", lambda e, i=stb[:, 1:2], o=stb[:, 2:3]:
                              e.reciprocal(o, i), waits=[rmw])
                        p.vsep()
                        w = [] if kv_guard[r] is None else [kv_guard[r]]
                        kvw = p.add("vector", lambda e, i=ps[0][:, 0:KVR],
                                    o=kvt[r][:], rs=stb[:, 2:3], g=g_bc[:]:
                                    e.scalar_tensor_tensor(o, i, rs, g,
                                                           ALU.mult, ALU.mult),
                                    waits=w, sig=p.VEC)
                        p.add("tensor", None, waits=[kvw])
                        tp_last = p.dma(
                            "sync",
                            bass.AP(kvn_fm, t * 128,
                                    [[2 * T, 128], [T, 2], [1, 128]]),
                            kvt[r][:], KVT[t % 2], waits=[kvw], transpose=True)
                        kv_guard[r] = tp_last
                        pump(1)
                    KVN_DONE = (KVT[0], KVT[0].v)
                    KVN_DONE2 = (KVT[1], KVT[1].v)

                    qc_guard = [None, None]
                    last_ev = None
                    for fc in range(4):
                        for tq in range(4):
                            k = fc * 4 + tq
                            bank = ps[1 + k % 2]
                            for c in range(DC):
                                w = []
                                if c == 0:
                                    if k == 0:
                                        w = [XN_DONE, XN_DONE2]
                                    elif qc_guard[k % 2] is not None:
                                        w = [qc_guard[k % 2]]
                                p.add("tensor", lambda e, a=bass.AP(
                                    w_q, c * QR + fc * 128,
                                    [[DC * QR, 128], [1, 128]]),
                                    b=bass.AP(xn_fm, c * T + tq * 512,
                                              [[DC * T, 128], [1, 512]]),
                                    o=bank[:], s_=(c == 0), sp=(c == DC - 1):
                                    e.matmul(o, a, b, start=s_, stop=sp),
                                    waits=w, sig=p.TEN if c == DC - 1 else None)
                            mmw = (p.TEN, p.TEN.v)
                            last_ev = p.add("vector", lambda e, i=bank[:],
                                            o=bass.AP(qc_fm, fc * T + tq * 512,
                                                      [[4 * T, 128], [1, 512]]):
                                            e.tensor_copy(o, i),
                                            waits=[mmw], sig=p.VEC)
                            qc_guard[k % 2] = last_ev
                            pump(1)
                    QC_DONE = last_ev

                    def rope_proj(wsb, nk, src, srcn, dst_tok, first_wait):
                        rw = None
                        for t in range(NTT):
                            r = t % 2
                            bank = ps[3 - t % 2]
                            for c in range(nk):
                                w = []
                                if c == 0:
                                    if t == 0:
                                        w = list(first_wait)
                                    else:
                                        w = [(p.VEC, p.VEC.v)]
                                p.add("tensor", lambda e, a=bass.AP(
                                    src, c * T + t * 128,
                                    [[srcn * T, 128], [1, 128]]),
                                    b=bass.AP(wsb, c * QR, [[nk * QR, 128], [1, QR]]),
                                    o=bank[:], s_=(c == 0), sp=(c == nk - 1):
                                    e.matmul(o, a, b, start=s_, stop=sp),
                                    waits=w, sig=p.TEN if c == nk - 1 else None)
                            mmw = (p.TEN, p.TEN.v)
                            cosb = bass.AP(cos_t, t * DR,
                                           [[NTT * DR, 128], [0, H], [1, DR]])
                            sinlo = bass.AP(sinp_t, t * DR,
                                            [[NTT * DR, 128], [0, H], [1, 16]])
                            sinhi = bass.AP(sinp_t, t * DR + 16,
                                            [[NTT * DR, 128], [0, H], [1, 16]])
                            p4 = bank[:].rearrange("p (h u j) -> p h u j",
                                                   h=H, u=2)
                            r4 = rt2[:].rearrange("p (h u j) -> p h u j",
                                                  h=H, u=2)
                            p.add("vector", lambda e,
                                  i=bank[:].rearrange("p (h j) -> p h j", h=H),
                                  cb=cosb,
                                  o=rt1[:].rearrange("p (h j) -> p h j", h=H):
                                  e.tensor_tensor(o, i, cb, ALU.mult), waits=[mmw])
                            p.add("vector", lambda e, i=p4[:, :, 1, :], sb4=sinlo,
                                  o=r4[:, :, 0, :]:
                                  e.tensor_tensor(o, i, sb4, ALU.mult))
                            p.add("vector", lambda e, i=p4[:, :, 0, :], sb4=sinhi,
                                  o=r4[:, :, 1, :]:
                                  e.tensor_tensor(o, i, sb4, ALU.mult))
                            p.vsep()
                            rw = p.add("vector", lambda e, i=rt1[:], j=rt2[:],
                                       o=dst_tok[:, t * 512:(t + 1) * 512]:
                                       e.tensor_tensor(o, i, j, ALU.add),
                                       sig=p.VEC)
                            p.add("tensor", None, waits=[(p.VEC, p.VEC.v)])
                            pump(1)
                        return rw

                    KR_DONE = rope_proj(w_kr, DC, xn_fm, DC, kr_tok,
                                        [XN_DONE, XN_DONE2, rtw])
                    QR_DONE = rope_proj(w_qr, 4, qc_fm, 4, qr_tok, [QC_DONE])
                    p.fence()
                    if s == 0:
                        tap("xnfm", xn_fm[:], [128, DC * T], BF16)
                        tap("kvnfm", kvn_fm[:], [128, 2 * T], BF16)
                        tap("qcfm", qc_fm[:], [128, 4 * T], BF16)
                        tap("krtok", kr_tok[:], [128, NTT * 512], BF16)
                        tap("qrtok", qr_tok[:], [128, NTT * 512], BF16)
                    p.flush()

                with ExitStack() as dst_:
                    dsb = lambda shape, d: dst_.enter_context(nc.sbuf_tensor(_nm(), shape, d))
                    xq_fm = seqst.enter_context(
                        nc.sbuf_tensor(_nm(), [128, DC * T], BF16))
                    O_fm = dsb([128, DC * T], BF16)

                    with ExitStack() as grp:
                        gsb = lambda shape, d: grp.enter_context(
                            nc.sbuf_tensor(_nm(), shape, d, side="right"))
                        wkd_sl = gsb([128, 2 * 256], BF16)
                        wqd_sl = gsb([128, 4 * 256], BF16)
                        wvd_sl = gsb([128, 2 * 256], BF16)
                        K_fm = gsb([128, 4 * T], BF16)
                        Q_fm = gsb([128, 4 * T], BF16)
                        V_g = gsb([128, NTT * 384], BF16)
                        PT = [gsb([128, 1024], BF16) for _ in range(2)]
                        ktr = [gsb([128, 512], BF16) for _ in range(2)]
                        r_sb = gsb([128, 4 * T], BF16)
                        if s == 0:
                            WDEC = p.dsem("WDEC")
                            KTP = [p.dsem("KTP0"), p.dsem("KTP1")]
                            sems_cache[3] = (WDEC, KTP)
                        (WDEC, KTP) = sems_cache[3]

                        for g in range(NG):
                            gw_ = [(p.TEN, p.TEN.v)]
                            wld = None
                            for (dsl, srch, nk) in ((wkd_sl, w_kd_h, 2),
                                                    (wqd_sl, w_qd_h, 4),
                                                    (wvd_sl, w_vd_h, 2)):
                                for c in range(nk):
                                    wld = p.dma("gpsimd",
                                                dsl[:, c * 256:(c + 1) * 256],
                                                srch[:, c, g * 256:(g + 1) * 256],
                                                WDEC, waits=gw_ + [CONSTS_G])
                                    gw_ = []
                            WSL = wld

                            for (ridx, rope_tok, dsl, nk, dsrc, srcn, dfm) in (
                                    (0, kr_tok, wkd_sl, 2, kvn_fm, 2, K_fm),
                                    (1, qr_tok, wqd_sl, 4, qc_fm, 4, Q_fm)):
                                for t in range(NTT):
                                    r = t % 2
                                    bank = ps[r]
                                    for c in range(nk):
                                        w = []
                                        if c == 0:
                                            w = [(p.VEC, p.VEC.v), WSL]
                                            if t == 0 and ridx == 0 and g == 0:
                                                w += [KVN_DONE, KVN_DONE2]
                                        p.add("tensor", lambda e, a=bass.AP(
                                            dsrc, c * T + t * 128,
                                            [[srcn * T, 128], [1, 128]]),
                                            b=dsl[:, c * 256:(c + 1) * 256],
                                            o=bank[:, 0:256], s_=(c == 0),
                                            sp=(c == nk - 1):
                                            e.matmul(o, a, b, start=s_, stop=sp),
                                            waits=w,
                                            sig=p.TEN if c == nk - 1 else None)
                                    mmw = (p.TEN, p.TEN.v)
                                    w = [mmw, (KTP[r], KTP[r].v)]
                                    p.add("vector", lambda e, i=bank[:, 0:256],
                                          o=bass.AP(ktr[r], 0,
                                                    [[512, 128], [128, 4], [1, 64]]):
                                          e.tensor_copy(o, i), waits=w)
                                    cw = p.add("vector", lambda e, i=bass.AP(
                                        rope_tok, t * 512 + g * 128,
                                        [[NTT * 512, 128], [1, 128]]),
                                        o=bass.AP(ktr[r], 64,
                                                  [[512, 128], [128, 4], [1, 32]]):
                                        e.tensor_copy(o, i), sig=p.VEC)
                                    p.dma("sync", bass.AP(
                                        dfm, t * 128,
                                        [[4 * T, 128], [T, 4], [1, 128]]),
                                        ktr[r][:], KTP[r], waits=[cw],
                                        transpose=True)
                                    pump(1)
                            KQ_DONE = [(KTP[0], KTP[0].v), (KTP[1], KTP[1].v)]

                            if g == 0:
                                for pr in range(2):
                                    p.add("vector", lambda e, o=bass.AP(
                                        V_g, pr * 192 + 64,
                                        [[NTT * 384, 128], [384, NTT], [1, 64]]):
                                        e.memset(o, 0.0))
                                    p.add("vector", lambda e, o=bass.AP(
                                        V_g, pr * 192 + 64 + 32 * pr,
                                        [[NTT * 384, 128], [384, NTT], [1, 1]]):
                                        e.memset(o, 1.0), sig=p.VEC)
                            STRIPS = (p.VEC, p.VEC.v)
                            for t in range(NTT):
                                bank = ps[t % 2]
                                for c in range(2):
                                    w = []
                                    if c == 0:
                                        w = [(p.VEC, p.VEC.v), WSL]
                                        if t == 0:
                                            w += [STRIPS, KVN_DONE, KVN_DONE2]
                                    p.add("tensor", lambda e, a=bass.AP(
                                        kvn_fm, c * T + t * 128,
                                        [[2 * T, 128], [1, 128]]),
                                        b=wvd_sl[:, c * 256:(c + 1) * 256],
                                        o=bank[:, 0:256], s_=(c == 0), sp=(c == 1):
                                        e.matmul(o, a, b, start=s_, stop=sp),
                                        waits=w, sig=p.TEN if c == 1 else None)
                                mmw = (p.TEN, p.TEN.v)
                                p.add("vector", lambda e,
                                      i=bank[:, 0:256].rearrange(
                                          "p (pr sd dh) -> p pr sd dh", pr=2, sd=2),
                                      o=bass.AP(V_g, t * 384,
                                                [[NTT * 384, 128], [192, 2],
                                                 [128, 2], [1, 64]]):
                                      e.tensor_copy(o, i), waits=[mmw], sig=p.VEC)
                                pump(1)
                            VG_DONE = (p.VEC, p.VEC.v)
                            p.fence()
                            if s == 0 and g == 0:
                                tap("kfm", K_fm[:], [128, 4 * T], BF16)
                                tap("qfm", Q_fm[:], [128, 4 * T], BF16)
                                tap("vg", V_g[:], [128, NTT * 384], BF16)

                            for hh in range(GH):
                                h = 4 * g + hh
                                vwin = (hh // 2) * 192 + (64 if (hh % 2) else 0)
                                prow = (64 + 32 * (hh // 2)) if hh % 2 == 0 \
                                    else (32 * (hh // 2))
                                for jb in range(4):
                                    ob = ps[2 + jb % 2]
                                    nk = 4 * jb + 4
                                    pv_bank_w = (p.VEC, p.VEC.v)
                                    half_guard = {}
                                    for ii in range(0, 4 * jb, 2):
                                        pb = (ii // 2) % 2
                                        sp_ = psS[pb]
                                        for d in range(2):
                                            i = ii + d
                                            hk = pb * 2 + d
                                            w = []
                                            if hk in half_guard:
                                                w = [half_guard[hk]]
                                            if ii == 0 and d == 0:
                                                w = w + KQ_DONE
                                            p.add("tensor", lambda e, a=bass.AP(
                                                K_fm, hh * T + i * 128,
                                                [[4 * T, 96], [1, 128]]),
                                                b=bass.AP(Q_fm, hh * T + jb * 512,
                                                          [[4 * T, 96], [1, 512]]),
                                                o=sp_[:, d * 512:(d + 1) * 512]:
                                                e.matmul(o, a, b, start=True,
                                                         stop=True),
                                                waits=w,
                                                sig=p.TEN if d == 1 else None)
                                        qkw = (p.TEN, p.TEN.v)
                                        slot = PT[pb]
                                        ew_ = p.add("scalar", lambda e,
                                                    i_=sp_[:, 0:1024],
                                                    o=slot[:, 0:1024],
                                                    b=cst[:, 0:1]:
                                                    e.activation(o, i_, AF.Exp,
                                                                 bias=b,
                                                                 scale=SCALE),
                                                    waits=[qkw], sig=p.SCA)
                                        half_guard[pb * 2] = ew_
                                        half_guard[pb * 2 + 1] = ew_
                                        for d in range(2):
                                            i = ii + d
                                            w = [ew_]
                                            if i == 0:
                                                w += [pv_bank_w, VG_DONE]
                                            p.add("tensor", lambda e, a=bass.AP(
                                                V_g, i * 384 + vwin,
                                                [[NTT * 384, 128], [1, 128]]),
                                                b=slot[:, d * 512:(d + 1) * 512],
                                                o=ob[:], s_=(i == 0):
                                                e.matmul(o, a, b, start=s_,
                                                         stop=False),
                                                waits=w)
                                    for m in range(4):
                                        i = 4 * jb + m
                                        pb = (i // 2) % 2
                                        dhalf = i % 2
                                        hk = pb * 2 + dhalf
                                        sp_h = psS[pb][:, dhalf * 512:
                                                       dhalf * 512 + 512]
                                        lo = 128 * m
                                        w = []
                                        if hk in half_guard:
                                            w = [half_guard[hk]]
                                        if i == 0:
                                            w = w + KQ_DONE
                                        qkw = p.add("tensor", lambda e, a=bass.AP(
                                            K_fm, hh * T + i * 128,
                                            [[4 * T, 96], [1, 128]]),
                                            b=bass.AP(Q_fm,
                                                      hh * T + jb * 512 + lo,
                                                      [[4 * T, 96], [1, 512 - lo]]),
                                            o=sp_h[:, lo:512]:
                                            e.matmul(o, a, b, start=True,
                                                     stop=True),
                                            waits=w, sig=p.TEN)
                                        slot = PT[pb]
                                        off = dhalf * 512
                                        ew_ = p.add("scalar", lambda e,
                                                    i_=sp_h[:, lo:512],
                                                    o=slot[:, off + lo:off + 512],
                                                    b=cst[:, 0:1]:
                                                    e.activation(o, i_, AF.Exp,
                                                                 bias=b,
                                                                 scale=SCALE),
                                                    waits=[qkw], sig=p.SCA)
                                        half_guard[hk] = ew_
                                        dep = p.add("vector", lambda e,
                                                    o=slot[:, off + lo:
                                                           off + lo + 128],
                                                    tr=tri[:]:
                                                    e.tensor_tensor(o, o, tr,
                                                                    ALU.mult),
                                                    waits=[ew_], sig=p.VEC)
                                        half_guard[hk] = dep
                                        w = [dep]
                                        if i == 0:
                                            w += [pv_bank_w, VG_DONE]
                                        p.add("tensor", lambda e, a=bass.AP(
                                            V_g, i * 384 + vwin,
                                            [[NTT * 384, 128], [1, 128]]),
                                            b=slot[:, off + lo:off + 512],
                                            o=ob[:, lo:512],
                                            s_=(i == 0), sp=(i == nk - 1):
                                            e.matmul(o, a, b, start=s_, stop=sp),
                                            waits=w,
                                            sig=p.TEN if i == nk - 1 else None)
                                    pvw = (p.TEN, p.TEN.v)
                                    orow = 0 if h % 2 == 0 else 64
                                    p.add("vector", lambda e,
                                        i=ob[orow:orow + 64, :],
                                        o=bass.AP(O_fm,
                                                  orow * DC * T + (h // 2) * T
                                                  + jb * 512,
                                                  [[DC * T, 64], [1, 512]]):
                                        e.tensor_copy(o, i), waits=[pvw])
                                    p.add("vector", lambda e,
                                        i=ob[prow:prow + 1, :],
                                        o=bass.AP(r_sb,
                                                  prow * 4 * T + g * T + jb * 512,
                                                  [[4 * T, 1], [1, 512]]):
                                        e.tensor_copy(o, i), sig=p.VEC)
                                    pump(2)
                            p.fence()
                        if s == 0:
                            RS1 = p.dsem("RS1")
                            sems_cache[4] = RS1
                        RS1 = sems_cache[4]
                        wf = p.dma("sync", r_fs[:], r_sb[:], RS1)
                        w2 = p.dma("sync",
                                   bass.AP(r_s, s * H * T,
                                           [[4 * 2048, 4], [2048, 4], [1, 2048]]),
                                   bass.AP(r_fs, 0,
                                           [[2048, 4], [32 * 4 * T, 4], [1, 2048]]),
                                   RS1, waits=[wf])
                        if s == 0 and DEBUG:
                            tap("rsb", r_sb[:], [128, 4 * T], BF16)
                        p.fence()
                        p.flush()

                    with ExitStack() as eph:
                        esb = lambda shape, d: eph.enter_context(nc.sbuf_tensor(_nm(), shape, d))
                        w_o = esb([128, DC * D], BF16)
                        WOL = p.dma("gpsimd", w_o[:],
                                    w_o_h[:].rearrange("p a b -> p (a b)"),
                                    WLATE, waits=[CONSTS_G])
                        R_rep = esb([128, T], F32)
                        xt2 = [esb([128, D], F32) for _ in range(2)]
                        x2t = [esb([128, D], F32) for _ in range(2)]
                        x2n = esb([128, D], F32)
                        xqt = [esb([128, D], BF16) for _ in range(2)]
                        qbE = QBufs(eph, nc, 512)
                        if s == 0:
                            tap("ofm_pre", O_fm[:], [128, DC * T], BF16)
                            RS2 = p.dsem("RS2")
                            RS3 = p.dsem("RS3")
                            RREP = [p.dsem("RREP0"), p.dsem("RREP1")]
                            X2W = [p.dsem("X2W0"), p.dsem("X2W1")]
                            XQT = [p.dsem("XQT0"), p.dsem("XQT1")]
                            sems_cache[1] = (RS2, RS3, RREP, X2W, XQT)
                        (RS2, RS3, RREP, X2W, XQT) = sems_cache[1]
                        rcp = esb([128, 256], BF16)
                        rcp2 = esb([128, 256], F32)

                        ldr = p.dma("sync", rcp[:],
                                    bass.AP(r_s, s * H * T, [[256, 128], [1, 256]]),
                                    RS2, waits=[w2])
                        rcw = p.add("vector", lambda e, i=rcp[:], o=rcp2[:]:
                                    e.reciprocal(o, i), waits=[ldr], sig=p.VEC)
                        str_ = p.dma("sync",
                                     bass.AP(r_s2, s * H * T, [[256, 128], [1, 256]]),
                                     rcp2[:], RS3, waits=[rcw])
                        if s == 0 and DEBUG:
                            th1 = nc.dram_tensor("tap_rs", [H, T], BF16,
                                                 kind="ExternalOutput")
                            th2 = nc.dram_tensor("tap_rs2", [H, T], F32,
                                                 kind="ExternalOutput")
                            p.fence()
                            p.dma("sync", th1[:].rearrange("a b -> (a b)"),
                                  bass.AP(r_s, s * H * T, [[1, H * T]]),
                                  p.dsem("TPRS"))
                            p.dma("sync", th2[:].rearrange("a b -> (a b)"),
                                  bass.AP(r_s2, s * H * T, [[1, H * T]]),
                                  p.dsem("TPRS2"))
                            p.fence()
                        ng = [None, None]
                        last_norm = None
                        for h in range(H):
                            half = (h % 2) * 64
                            w = [str_]
                            if ng[h % 2] is not None:
                                w.append(ng[h % 2])
                            rep = p.dma("sync",
                                        bass.AP(R_rep, half * T, [[T, 64], [1, T]]),
                                        bass.AP(r_s2,
                                                (s * H + 4 * (h // 4)
                                                 + SLOT[h % 4]) * T,
                                                [[0, 64], [1, T]]),
                                        RREP[h % 2], waits=w)
                            nw = p.add("vector", lambda e, o=bass.AP(
                                O_fm, half * DC * T + (h // 2) * T,
                                [[DC * T, 64], [1, T]]),
                                rp=bass.AP(R_rep, half * T, [[T, 64], [1, T]]):
                                e.tensor_tensor(o, o, rp, ALU.mult),
                                waits=[rep], sig=p.VEC)
                            ng[h % 2] = nw
                            last_norm = nw
                        if s == 0:
                            p.fence()
                            tap("ofm_post", O_fm[:], [128, DC * T], BF16)

                        xt_guard = [None, None]
                        x2_guard = [None, None]
                        xq_guard = [None, None]
                        tp_last = None
                        for t in range(NTT):
                            r = t % 2
                            w = [] if xt_guard[r] is None else [xt_guard[r]]
                            ldx = p.dma("sync", xt2[r][:],
                                        xh[s, t * 128:(t + 1) * 128, :], LDX[r],
                                        waits=w)
                            evs = None
                            for f2 in range(2):
                                bank = ps[f2]
                                for c in range(DC):
                                    w = []
                                    if c == 0:
                                        w = [(p.VEC, p.VEC.v)]
                                        if t == 0 and f2 == 0:
                                            w.append(last_norm)
                                            w.append(WOL)
                                    p.add("tensor", lambda e, a=bass.AP(
                                        O_fm, c * T + t * 128,
                                        [[DC * T, 128], [1, 128]]),
                                        b=bass.AP(w_o, c * D + f2 * 512,
                                                  [[DC * D, 128], [1, 512]]),
                                        o=bank[:], s_=(c == 0), sp=(c == DC - 1):
                                        e.matmul(o, a, b, start=s_, stop=sp),
                                        waits=w,
                                        sig=p.TEN if c == DC - 1 else None)
                                mmw = (p.TEN, p.TEN.v)
                                w = [mmw, ldx]
                                if x2_guard[r] is not None and f2 == 0:
                                    w += x2_guard[r]
                                evs = p.add("vector", lambda e, i=bank[:],
                                            j=xt2[r][:, f2 * 512:(f2 + 1) * 512],
                                            o=x2t[r][:, f2 * 512:(f2 + 1) * 512]:
                                            e.tensor_tensor(o, i, j, ALU.add),
                                            waits=w, sig=p.VEC)
                            xt_guard[r] = evs
                            x2w = p.dma("sync", x2_s[s, t * 128:(t + 1) * 128, :],
                                        x2t[r][:], X2W[r], waits=[evs])
                            lw_, sqw = ln_unit(p, x2t[r][:], stt_[r], x2n[:],
                                               junk, cst, evs)
                            x2_guard[r] = [lw_, x2w, sqw]
                            qw = None
                            for hf in range(2):
                                pw = [lw_]
                                if hf == 0 and xq_guard[r] is not None:
                                    pw.append(xq_guard[r])
                                qw = quant_unit(p, qbE,
                                                x2n[:, hf * 512:(hf + 1) * 512],
                                                32, xqt[r][:, hf * 512:(hf + 1) * 512],
                                                pre_waits=pw)
                            tp_last = p.dma(
                                "sync",
                                bass.AP(xq_fm, t * 128,
                                        [[DC * T, 128], [T, DC], [1, 128]]),
                                xqt[r][:], XQT[t % 2], waits=[qw], transpose=True)
                            xq_guard[r] = tp_last
                            pump(2)
                        XQ_DONE = (XQT[0], XQT[0].v)
                        XQ_DONE2 = (XQT[1], XQT[1].v)
                        if prep is not None:
                            for _ in prep:
                                pass
                        p.fence()
                        if s == 0:
                            tap("xqfm", xq_fm[:], [128, DC * T], BF16)
                        p.flush()

            prost.close()
            with ExitStack() as ffn:
                fsb = lambda shape, d: ffn.enter_context(nc.sbuf_tensor(_nm(), shape, d))
                wgq = fsb([128, DC * FF], BF16)
                wuq = fsb([128, DC * FF], BF16)
                h_tok = fsb([128, 4 * FF], BF16)
                hqT = fsb([128, FFC * 512], BF16)
                g_sb = [fsb([128, 512], BF16) for _ in range(2)]
                wdq = [fsb([128, D], BF16) for _ in range(4)]
                x2r = [fsb([128, D], F32) for _ in range(2)]
                yob = [fsb([128, 512], F32) for _ in range(2)]
                qbF = QBufs(ffn, nc, 512)
                if s == 0:
                    WGL = p.dsem("WGL")
                    WDQ = [p.dsem(f"WDQ{i}") for i in range(4)]
                    HQT = p.dsem("HQT")
                    YST = [p.dsem("YST0"), p.dsem("YST1")]
                    sems_cache[2] = (WGL, WDQ, HQT, YST)
                (WGL, WDQ, HQT, YST) = sems_cache[2]

                ldg = p.dma("sync", wgq[:],
                            bass.AP(wgq_s, 0, [[FF, 128], [128 * FF, DC], [1, FF]]),
                            WGL)
                ldu = p.dma("sync", wuq[:],
                            bass.AP(wuq_s, 0, [[FF, 128], [128 * FF, DC], [1, FF]]),
                            WGL)
                NS = [512] * 5 + [256]
                x2r_guard = [None, None]
                wd_guard = [None] * 4
                prev_down = None
                for ts4 in range(4):
                    for nsi in range(6):
                        nw_ = NS[nsi]
                        n0 = nsi * 512
                        for tt in range(4):
                            t = ts4 * 4 + tt
                            gb = ps[tt % 2]
                            ub = ps[2 + tt % 2]
                            for c in range(DC):
                                w = []
                                if c == 0:
                                    w = [(p.SCA, p.SCA.v), (p.VEC, p.VEC.v)]
                                    if ts4 == 0 and nsi == 0 and tt == 0:
                                        w += [XQ_DONE, XQ_DONE2, ldg, ldu]
                                p.add("tensor", lambda e, a=bass.AP(
                                    xq_fm, c * T + t * 128,
                                    [[DC * T, 128], [1, 128]]),
                                    b=bass.AP(wgq, c * FF + n0,
                                              [[DC * FF, 128], [1, nw_]]),
                                    o=gb[:, 0:nw_], s_=(c == 0), sp=(c == DC - 1):
                                    e.matmul(o, a, b, start=s_, stop=sp),
                                    waits=w, sig=p.TEN if c == DC - 1 else None)
                            gmw = (p.TEN, p.TEN.v)
                            for c in range(DC):
                                w = [(p.VEC, p.VEC.v)] if c == 0 else []
                                p.add("tensor", lambda e, a=bass.AP(
                                    xq_fm, c * T + t * 128,
                                    [[DC * T, 128], [1, 128]]),
                                    b=bass.AP(wuq, c * FF + n0,
                                              [[DC * FF, 128], [1, nw_]]),
                                    o=ub[:, 0:nw_], s_=(c == 0), sp=(c == DC - 1):
                                    e.matmul(o, a, b, start=s_, stop=sp),
                                    waits=w, sig=p.TEN if c == DC - 1 else None)
                            umw = (p.TEN, p.TEN.v)
                            sw = p.add("scalar", lambda e, i=gb[:, 0:nw_],
                                       o=g_sb[tt % 2][:, 0:nw_]:
                                       e.activation(o, i, AF.Silu),
                                       waits=[gmw], sig=p.SCA)
                            hm_w = [sw, umw]
                            if nsi == 0:
                                hm_w.append((HQT, HQT.v))
                            p.add("vector", lambda e, i=g_sb[tt % 2][:, 0:nw_],
                                  j=ub[:, 0:nw_],
                                  o=h_tok[:, tt * FF + n0: tt * FF + n0 + nw_]:
                                  e.tensor_tensor(o, i, j, ALU.mult),
                                  waits=hm_w, sig=p.VEC)
                    H_DONE = (p.VEC, p.VEC.v)
                    hq_last = None
                    QCH = [512] * 5 + [256]
                    for tt in range(4):
                        qw = None
                        q0 = 0
                        for qt in range(6):
                            pw = []
                            if tt == 0 and qt == 0:
                                pw = [H_DONE]
                            qw = quant_unit(
                                p, qbF,
                                h_tok[:, tt * FF + q0: tt * FF + q0 + QCH[qt]],
                                QCH[qt] // 16,
                                h_tok[:, tt * FF + q0: tt * FF + q0 + QCH[qt]],
                                pre_waits=pw)
                            q0 += QCH[qt]
                        tq_w = [qw]
                        if prev_down is not None:
                            tq_w.append(prev_down)
                        hq_last = p.dma(
                            "sync",
                            bass.AP(hqT, tt * 128,
                                    [[FFC * 512, 128], [512, FFC], [1, 128]]),
                            h_tok[:, tt * FF: tt * FF + FF], HQT,
                            waits=tq_w, transpose=True)
                    HQ_DONE = hq_last
                    for nci in range(FFC):
                        r4 = nci % 4
                        w = []
                        if wd_guard[r4] is not None:
                            w = [wd_guard[r4]]
                        ldw = p.dma("sync", wdq[r4][:],
                                    wdq_s[nci * 128:(nci + 1) * 128, :], WDQ[r4],
                                    waits=w)
                        for tt in range(4):
                            for f2 in range(2):
                                bank = ps[tt * 2 + f2]
                                w = []
                                if tt == 0 and f2 == 0:
                                    w = [ldw]
                                    if nci == 0:
                                        w += [HQ_DONE, (p.VEC, p.VEC.v)]
                                p.add("tensor", lambda e, a=bass.AP(
                                    hqT, nci * 512 + tt * 128,
                                    [[FFC * 512, 128], [1, 128]]),
                                    b=wdq[r4][:, f2 * 512:(f2 + 1) * 512],
                                    o=bank[:], s_=(nci == 0), sp=(nci == FFC - 1):
                                    e.matmul(o, a, b, start=s_, stop=sp),
                                    waits=w,
                                    sig=p.TEN if (tt == 3 and f2 == 1) else None)
                        wd_guard[r4] = (p.TEN, p.TEN.v)
                    DOWN_DONE = (p.TEN, p.TEN.v)
                    prev_down = DOWN_DONE
                    yst = None
                    for tt in range(4):
                        t = ts4 * 4 + tt
                        r = tt % 2
                        w = [] if x2r_guard[r] is None else [x2r_guard[r]]
                        ldx2 = p.dma("sync", x2r[r][:],
                                     x2_s[s, t * 128:(t + 1) * 128, :], LDX[r],
                                     waits=w)
                        for f2 in range(2):
                            bank = ps[tt * 2 + f2]
                            yw = p.add("vector", lambda e, i=bank[:],
                                       j=x2r[r][:, f2 * 512:(f2 + 1) * 512],
                                       o=yob[f2][:]:
                                       e.tensor_tensor(o, i, j, ALU.add),
                                       waits=[DOWN_DONE, ldx2,
                                              (YST[f2], YST[f2].v)], sig=p.VEC)
                            x2r_guard[r] = yw
                            yst = p.dma(
                                "sync",
                                yh[s, t * 128:(t + 1) * 128,
                                   f2 * 512:(f2 + 1) * 512],
                                yob[f2][:], YST[f2], waits=[yw])
                    if ts4 == 3:
                        p.fence()
                p.flush()

    p.ops["sync"].append(([(s_.h, s_.v) for s_ in p.dma_sems], None, [], False))
    p.flush()
    st.close()
    return nc


def _rope_tables():
    inv = 1.0 / 10000.0 ** (np.arange(0, DR, 2, dtype=np.float32) / DR)
    t = np.arange(T, dtype=np.float32)
    freqs = np.outer(t, inv)
    emb = np.concatenate([freqs, freqs], -1)
    cos = np.cos(emb).astype(np.float32)
    sinp = np.sin(emb).astype(np.float32)
    sinp[:, :DR // 2] *= -1.0
    cos_t = np.ascontiguousarray(cos.reshape(NTT, 128, DR).transpose(1, 0, 2))
    sinp_t = np.ascontiguousarray(sinp.reshape(NTT, 128, DR).transpose(1, 0, 2))
    return cos_t, sinp_t


def _rope_perm():
    perm = np.zeros(QR, dtype=np.int64)
    for rc in range(4):
        for m in range(4):
            h = 4 * rc + m
            sl = SLOT[m]
            perm[rc * 128 + sl * 32: rc * 128 + sl * 32 + 32] = \
                np.arange(h * 32, h * 32 + 32)
    return perm


def _pack_km(w):
    K, F = w.shape
    return np.ascontiguousarray(w.reshape(K // 128, 128, F).transpose(1, 0, 2))


def _pack_nk(w):
    N, K = w.shape
    return np.ascontiguousarray(w.reshape(N // 128, 128, K).transpose(1, 0, 2))


_CACHE = {}


def kernel(L0_attn_x, L0_attn_W_kv, L0_attn_g_kv, L0_attn_W_kdec, L0_attn_W_vdec,
           L0_attn_W_q, L0_attn_W_qdec, L0_attn_W_krope, L0_attn_W_qrope,
           L0_attn_W_o, L0_ffn_w_gate, L0_ffn_w_up, L0_ffn_w_down):
    x = np.ascontiguousarray(np.asarray(L0_attn_x, dtype=np.float32))
    perm = _rope_perm()
    cos_t, sinp_t = _rope_tables()
    tri = np.triu(np.ones((128, 128), dtype=np.float32))
    g_bc = np.ascontiguousarray(np.broadcast_to(
        np.asarray(L0_attn_g_kv, np.float32)[None, :], (128, KVR)))
    cst = np.zeros((128, 8), dtype=np.float32)
    cst[:, 0] = CSHIFT
    cst[:, 1] = 1e-5
    cst[:, 2] = 1e-6

    wmaps = {
        "w_kv": _pack_km(np.asarray(L0_attn_W_kv, np.float32)),
        "w_q": _pack_km(np.asarray(L0_attn_W_q, np.float32)),
        "w_kr": _pack_km(np.asarray(L0_attn_W_krope, np.float32)),
        "w_qr": _pack_km(np.asarray(L0_attn_W_qrope, np.float32)),
        "w_kd": _pack_km(np.asarray(L0_attn_W_kdec, np.float32)),
        "w_qd": _pack_km(np.asarray(L0_attn_W_qdec, np.float32)),
        "w_vd": _pack_km(np.asarray(L0_attn_W_vdec, np.float32)),
        "w_o": _pack_km(np.asarray(L0_attn_W_o, np.float32)),
        "wg": _pack_nk(np.asarray(L0_ffn_w_gate, np.float32)),
        "wu": _pack_nk(np.asarray(L0_ffn_w_up, np.float32)),
        "wd": _pack_nk(np.asarray(L0_ffn_w_down, np.float32)),
        "cos_t": cos_t, "sinp_t": sinp_t, "tri": tri, "g_bc": g_bc, "cst": cst,
    }

    if "nc" not in _CACHE:
        _CACHE["nc"] = build_program()
    nc = _CACHE["nc"]

    in_maps = []
    for c in range(NCORES):
        m = dict(wmaps)
        m["x"] = np.ascontiguousarray(x[2 * c: 2 * c + 2])
        in_maps.append(m)
    res = run_bass_kernel_spmd(nc, in_maps, core_ids=list(range(NCORES)))
    out = np.concatenate([np.asarray(res.results[c]["y"]) for c in range(NCORES)],
                         axis=0)
    return out.astype(np.float32)


# version 13
# speedup vs baseline: 1.1408x; 1.1408x over previous
import math
import numpy as np
import concourse.bass as bass
import concourse.mybir as mybir
from concourse.bass_utils import run_bass_kernel_spmd

F32 = mybir.dt.float32
BF16 = mybir.dt.bfloat16
AF = mybir.ActivationFunctionType
ALU = mybir.AluOpType
AX = mybir.AxisListType

NCORES = 8
SEQS = 2
T = 2048
NTT = 16
D = 1024
DC = 8
H = 16
DR = 32
KVR = 256
QR = 512
FF = 2816
FFC = 22
NG = 4
GH = 4
SCALE = 1.0 / math.sqrt(96.0)
CSHIFT = -12.0
MH = 6291456.0
M1 = 12582912.0
M2 = 25165824.0
SLOT = [2, 0, 3, 1]
DEBUG = False

ENGS = ("sync", "gpsimd", "tensor", "vector", "scalar")

_NAME_N = [0]


def _nm(pfx="t"):
    _NAME_N[0] += 1
    return f"{pfx}{_NAME_N[0]}"



class _Sem:
    __slots__ = ("h", "v")

    def __init__(self, h):
        self.h = h
        self.v = 0


class Prog:
    def __init__(self, nc, stack):
        self.nc = nc
        self.ops = {e: [] for e in ENGS}
        self.last_inst = {e: None for e in ENGS}
        self.TEN = _Sem(stack.enter_context(nc.semaphore("sTEN")))
        self.VEC = _Sem(stack.enter_context(nc.semaphore("sVEC")))
        self.SCA = _Sem(stack.enter_context(nc.semaphore("sSCA")))
        self.GPS = _Sem(stack.enter_context(nc.semaphore("sGPS")))
        self.FEN = _Sem(stack.enter_context(nc.semaphore("sFEN")))
        self._stack = stack
        self.dma_sems = []
        self._fence_marks = {}
        self.esem = {"tensor": self.TEN, "vector": self.VEC,
                     "scalar": self.SCA, "gpsimd": self.GPS}

    def dsem(self, name):
        s = _Sem(self._stack.enter_context(self.nc.semaphore(name)))
        self.dma_sems.append(s)
        return s

    def add(self, eng, emit, waits=(), sig=None, dma=None):
        w = [(s.h, v) for (s, v) in waits if s is not None]
        snap = None
        incs = []
        if sig is not None:
            sig.v += 1
            incs.append((sig.h, 1))
            snap = (sig, sig.v)
        if dma is not None:
            dma.v += 16
            incs.append((dma.h, 16))
            snap = (dma, dma.v)
        self.ops[eng].append((w, emit, incs, dma is not None))
        return snap

    def dma(self, eng, out_ap, in_ap, sem, waits=(), transpose=False):

        def emit(e, o=out_ap, i=in_ap, tp=transpose):
            if tp:
                return e.dma_start_transpose(o, i)
            return e.dma_start(o, i)

        return self.add(eng, emit, waits=waits, dma=sem)

    def _ensure_signaled(self, e):
        sem = self.esem[e]
        for op in reversed(self.ops[e]):
            w, emit, incs, isdma = op
            if emit is None or isdma:
                continue
            for (h, amt) in incs:
                if h is sem.h:
                    return
            sem.v += 1
            incs.append((sem.h, 1))
            return

    def vsep(self):
        self.ops["vector"].append(([], lambda e: e.drain(), [], False))

    def fence(self):
        for e in ("tensor", "vector", "scalar", "gpsimd"):
            self._ensure_signaled(e)
        waits = [(self.TEN.h, self.TEN.v), (self.VEC.h, self.VEC.v),
                 (self.SCA.h, self.SCA.v), (self.GPS.h, self.GPS.v)]
        for s in self.dma_sems:
            if self._fence_marks.get(id(s)) != s.v:
                waits.append((s.h, s.v))
                self._fence_marks[id(s)] = s.v
        for e in ENGS:
            self.ops[e].append((list(waits), None, [], False))

    def flush(self):
        with self.nc.Block() as block:
            for engname in ENGS:
                lst = self.ops[engname]
                if not lst:
                    continue

                def body(eng, mylst=lst, en=engname):
                    for (w, emit, incs, isdma) in mylst:
                        for (h, v) in w:
                            eng.wait_ge(h, v)
                        inst = emit(eng) if emit is not None else None
                        if inst is not None:
                            self.last_inst[en] = inst
                        if incs:
                            tgt = inst if inst is not None else self.last_inst[en]
                            if tgt is None:
                                for (h, amt) in incs:
                                    eng.sem_inc(h, amt)
                            else:
                                for (h, amt) in incs:
                                    tgt.then_inc(h, amt)

                getattr(block, engname)(body)
        self.ops = {e: [] for e in ENGS}


class QBufs:

    def __init__(self, stack, nc, qw):
        sb = lambda shape, d: stack.enter_context(nc.sbuf_tensor(_nm(), shape, d))
        self.qw = qw
        nb = qw // 16
        self.Q1 = sb([128, qw], F32)
        self.Q2 = sb([128, qw], F32)
        self.Q3 = sb([128, qw], F32)
        self.Q4 = sb([128, qw], F32)
        self.Q5 = sb([128, qw], F32)
        self.Qs = sb([128, qw], BF16)
        self.sm = sb([128, 3 * nb], F32)
        self.nb = nb


def quant_unit(p, qb, in_ap, nb, out_ap, pre_waits=()):
    n = nb * 16
    NB = qb.nb
    bm = qb.sm[:, 0:nb]
    sS = qb.sm[:, NB:NB + nb]
    sinv = qb.sm[:, 2 * NB:2 * NB + nb]
    in3 = in_ap.rearrange("p (b k) -> p b k", k=16)
    p.add("vector", lambda e, i=in3, o=bm.rearrange("p (b o) -> p b o", o=1):
          e.tensor_reduce(o, i, AX.X, ALU.max, apply_absolute_value=True),
          waits=pre_waits)
    p.vsep()
    p.add("vector", lambda e, i=bm: e.tensor_single_scalar(i, i, 6e-12, ALU.max))
    p.vsep()
    p.add("vector", lambda e, i=bm, o=sS:
          e.tensor_single_scalar(o, i, 1.0 / 6.0, ALU.mult))
    p.vsep()
    p.add("vector", lambda e, i=sS, o=sinv: e.reciprocal(o, i))
    p.vsep()
    sinv_b = bass.AP(qb.sm, 2 * NB, [[3 * NB, 128], [1, nb], [0, 16]])
    s_b = bass.AP(qb.sm, NB, [[3 * NB, 128], [1, nb], [0, 16]])
    z = qb.Q1[:, 0:n]
    a = qb.Q2[:, 0:n]
    zw = p.add("vector", lambda e, i=in3, o=z.rearrange("p (b k) -> p b k", k=16),
               sv=sinv_b: e.tensor_tensor(o, i, sv, ALU.mult), sig=p.VEC)
    aw = p.add("scalar", lambda e, i=z, o=a: e.activation(o, i, AF.Abs),
               waits=[zw], sig=p.SCA)
    gw = p.add("scalar", lambda e, i=z, o=qb.Qs[:, 0:n]:
               e.activation(o, i, AF.Sign), sig=p.SCA)
    p.add("vector", lambda e, i=a, o=qb.Q5[:, 0:n]:
          e.tensor_scalar(o, i, MH, 2.0 + MH, ALU.add, ALU.min), waits=[aw])
    p.add("vector", lambda e, i=a, o=qb.Q3[:, 0:n]:
          e.tensor_scalar(o, i, M1, 2.0 + M1, ALU.add, ALU.max))
    p.add("vector", lambda e, i=qb.Q3[:, 0:n]:
          e.tensor_scalar(i, i, 4.0 + M1, M1 + 2.0, ALU.min, ALU.subtract))
    p.add("vector", lambda e, i=a, o=qb.Q4[:, 0:n]:
          e.tensor_scalar(o, i, M2, 4.0 + M2, ALU.add, ALU.max))
    p.add("vector", lambda e, i=qb.Q4[:, 0:n]:
          e.tensor_scalar(i, i, 6.0 + M2, M2 + 4.0, ALU.min, ALU.subtract))
    p.add("vector", lambda e, i=qb.Q5[:, 0:n], j=qb.Q3[:, 0:n], o=z:
          e.scalar_tensor_tensor(o, i, MH, j, ALU.subtract, ALU.add),
          waits=[gw])
    p.add("vector", lambda e, i=z, j=qb.Q4[:, 0:n]:
          e.tensor_tensor(i, i, j, ALU.add))
    p.add("vector", lambda e, i=z, j=qb.Qs[:, 0:n]:
          e.tensor_tensor(i, i, j, ALU.mult))
    w = p.add("vector",
              lambda e, i=z.rearrange("p (b k) -> p b k", k=16),
              o=out_ap.rearrange("p (b k) -> p b k", k=16), sv=s_b:
              e.tensor_tensor(o, i, sv, ALU.mult), sig=p.VEC)
    return w


def ln_unit(p, src_ap, stb, out_ap, junk, cst, ld_wait, extra_write_waits=()):
    sqw = p.add("scalar", lambda e, i=src_ap, o=junk[:, 0:D], ac=stb[:, 0:1]:
                e.activation(o, i, AF.Square, accum_out=ac),
                waits=[ld_wait], sig=p.SCA)
    p.add("vector", lambda e, i=src_ap, o=stb[:, 1:2]:
          e.reduce_sum(o, i, AX.X), waits=[ld_wait])
    p.vsep()
    p.add("vector", lambda e, i=stb[:, 1:2], o=stb[:, 2:3]:
          e.tensor_single_scalar(o, i, 1.0 / D, ALU.mult))
    p.vsep()
    p.add("vector", lambda e, i=stb[:, 2:3], o=stb[:, 3:4]:
          e.tensor_tensor(o, i, i, ALU.mult))
    p.vsep()
    vw = p.add("vector", lambda e, i=stb[:, 0:1], j=stb[:, 3:4], o=stb[:, 4:5]:
               e.scalar_tensor_tensor(o, i, 1.0 / D, j, ALU.mult, ALU.subtract),
               waits=[sqw], sig=p.VEC)
    sw = p.add("scalar", lambda e, i=stb[:, 4:5], o=stb[:, 5:6], b=cst[:, 1:2]:
               e.activation(o, i, AF.Sqrt, bias=b), waits=[vw], sig=p.SCA)
    p.add("vector", lambda e, i=stb[:, 5:6], o=stb[:, 6:7]:
          e.reciprocal(o, i), waits=[sw])
    p.vsep()
    xw = p.add("vector", lambda e, i=src_ap, o=out_ap, m=stb[:, 2:3],
               rs=stb[:, 6:7]:
               e.tensor_scalar(o, i, m, rs, ALU.subtract, ALU.mult),
               waits=list(extra_write_waits), sig=p.VEC)
    return xw, sqw


def build_program():
    from contextlib import ExitStack
    nc = bass.Bass("TRN2", target_bir_lowering=False)
    st = ExitStack()

    dt = nc.dram_tensor
    xh = dt("x", [SEQS, T, D], F32, kind="ExternalInput")
    yh = dt("y", [SEQS, T, D], F32, kind="ExternalOutput")
    w_kv_h = dt("w_kv", [128, DC, KVR], F32, kind="ExternalInput")
    w_q_h = dt("w_q", [128, DC, QR], F32, kind="ExternalInput")
    w_kr_h = dt("w_kr", [128, DC, QR], F32, kind="ExternalInput")
    w_qr_h = dt("w_qr", [128, 4, QR], F32, kind="ExternalInput")
    w_kd_h = dt("w_kd", [128, 2, D], F32, kind="ExternalInput")
    w_qd_h = dt("w_qd", [128, 4, D], F32, kind="ExternalInput")
    w_vd_h = dt("w_vd", [128, 2, D], F32, kind="ExternalInput")
    w_o_h = dt("w_o", [128, DC, D], F32, kind="ExternalInput")
    wg_h = dt("wg", [128, FFC, D], F32, kind="ExternalInput")
    wu_h = dt("wu", [128, FFC, D], F32, kind="ExternalInput")
    wd_h = dt("wd", [128, DC, FF], F32, kind="ExternalInput")
    cos_h = dt("cos_t", [128, NTT, DR], F32, kind="ExternalInput")
    sinp_h = dt("sinp_t", [128, NTT, DR], F32, kind="ExternalInput")
    tri_h = dt("tri", [128, 128], F32, kind="ExternalInput")
    g_h = dt("g_bc", [128, KVR], F32, kind="ExternalInput")
    cst_h = dt("cst", [128, 8], F32, kind="ExternalInput")

    wgq_s = dt("wgq_s", [D, FF], BF16, kind="Internal")
    wuq_s = dt("wuq_s", [D, FF], BF16, kind="Internal")
    wdq_s = dt("wdq_s", [FF, D], BF16, kind="Internal")
    x2_s = dt("x2_s", [SEQS, T, D], F32, kind="Internal")
    r_s = dt("r_s", [SEQS, H, T], BF16, kind="Internal")
    r_s2 = dt("r_s2", [SEQS, H, T], F32, kind="Internal")

    p = Prog(nc, st)
    sb = lambda shape, d: st.enter_context(nc.sbuf_tensor(_nm(), shape, d))

    def tap(name, src_ap, shape, dtyp):
        if not DEBUG:
            return
        th = nc.dram_tensor("tap_" + name, shape, dtyp, kind="ExternalOutput")
        p.fence()
        p.dma("sync", th[:].rearrange("a b -> (a b)") if len(shape) == 2 else th[:],
              src_ap, p.dsem("TAP" + name))
        p.fence()

    tri = sb([128, 128], BF16)
    g_bc = sb([128, KVR], F32)
    cst = sb([128, 8], F32)
    junk = sb([128, D], F32)
    stt_ = [sb([128, 8], F32) for _ in range(2)]
    psA = st.enter_context(nc.psum_tensor(_nm("psA"), [128, 1024], F32))
    psB = st.enter_context(nc.psum_tensor(_nm("psB"), [128, 1024], F32))
    ps = [st.enter_context(nc.psum_tensor(_nm("ps"), [128, 512], F32)) for _ in range(4)]
    ps = ps + [psA[:, 0:512], psA[:, 512:1024], psB[:, 0:512], psB[:, 512:1024]]
    psS = [psA, psB]

    CONG = p.dsem("CONG")
    CON = p.dsem("CON")
    CONSTS_G = p.dma("gpsimd", tri[:], tri_h[:], CONG)
    p.dma("sync", g_bc[:], g_h[:], CON)
    CONSTS = p.dma("sync", cst[:], cst_h[:], CON)

    with ExitStack() as pro:
        psb = lambda shape, d: pro.enter_context(nc.sbuf_tensor(_nm(), shape, d))
        wtT = psb([128, FFC * D], BF16)
        wtin = [psb([128, FF], F32) for _ in range(2)]
        wtq = [psb([128, FF], BF16) for _ in range(2)]
        qbP = QBufs(pro, nc, 1408)
        WIN = [p.dsem("WIN0"), p.dsem("WIN1")]
        TPS = [p.dsem("TPS0"), p.dsem("TPS1")]
        STS = p.dsem("STS")

        def prep_weight(src_h, ntiles, K, store_ap, dst_fn):
            guards_in = [None, None]
            guards_q = [None, None]
            for nt in range(ntiles):
                r = nt % 2
                w = [CONSTS_G]
                if guards_in[r] is not None:
                    w.append(guards_in[r])
                ld = p.dma("gpsimd", wtin[r][:, 0:K], src_h[:, nt, :], WIN[r],
                           waits=w)
                qw = None
                pw = [ld]
                if guards_q[r] is not None:
                    pw.append(guards_q[r])
                if K <= 1408:
                    qw = quant_unit(p, qbP, wtin[r][:, 0:K], K // 16,
                                    wtq[r][:, 0:K], pre_waits=pw)
                else:
                    for hf in range(K // 1408):
                        qw = quant_unit(p, qbP,
                                        wtin[r][:, hf * 1408:(hf + 1) * 1408],
                                        88, wtq[r][:, hf * 1408:(hf + 1) * 1408],
                                        pre_waits=pw if hf == 0 else ())
                guards_in[r] = qw
                tp = p.dma("sync", dst_fn(nt), wtq[r][:, 0:K], TPS[r],
                           waits=[qw], transpose=True)
                guards_q[r] = tp
            return p.dma("sync", store_ap, wtT[:], STS,
                         waits=[(TPS[0], TPS[0].v), (TPS[1], TPS[1].v)])

        def g_dst(nt):
            return bass.AP(wtT, nt * 128, [[FFC * D, 128], [FF, DC], [1, 128]])

        def d_dst(nt):
            return bass.AP(wtT, nt * 128, [[FFC * D, 128], [D, FFC], [1, 128]])

        gq_ap = bass.AP(wgq_s, 0, [[FF, 128], [128 * FF, DC], [1, FF]])
        uq_ap = bass.AP(wuq_s, 0, [[FF, 128], [128 * FF, DC], [1, FF]])
        dq_ap = bass.AP(wdq_s, 0, [[D, 128], [128 * D, FFC], [1, D]])
        prep_weight(wg_h, FFC, D, gq_ap, g_dst)
        p.fence()
        prep_weight(wu_h, FFC, D, uq_ap, g_dst)
        p.fence()
        prep_weight(wd_h, DC, FF, dq_ap, d_dst)
        p.fence()
        p.flush()

    sems_cache = {}
    for s in range(SEQS):
        with ExitStack() as seqst:
            with ExitStack() as bst:
                bsb = lambda shape, d: bst.enter_context(
                    nc.sbuf_tensor(_nm(), shape, d, side="right"))
                kvn_fm = bsb([128, 2 * T], BF16)
                qc_fm = bsb([128, 4 * T], BF16)
                kr_tok = bsb([128, NTT * 512], BF16)
                qr_tok = bsb([128, NTT * 512], BF16)

                with ExitStack() as cst1:
                    csb = lambda shape, d: cst1.enter_context(
                        nc.sbuf_tensor(_nm(), shape, d, side="right"))
                    xn_fm = csb([128, DC * T], BF16)
                    w_kv = csb([128, DC * KVR], BF16)
                    w_q = csb([128, DC * QR], BF16)
                    w_kr = csb([128, DC * QR], BF16)
                    w_qr = csb([128, 4 * QR], BF16)
                    cos_t = csb([128, NTT * DR], F32)
                    sinp_t = csb([128, NTT * DR], F32)
                    if s == 0:
                        ROPE_T = p.dsem("ROPET")
                        LDX = [p.dsem("LDX0"), p.dsem("LDX1")]
                        TPN = [p.dsem("TPN0"), p.dsem("TPN1")]
                        KVT = [p.dsem("KVT0"), p.dsem("KVT1")]
                        RPT = [p.dsem("RPT0"), p.dsem("RPT1")]
                        WEARLY = p.dsem("WEARLY")
                        WLATE = p.dsem("WLATE")
                        sems_cache[0] = (ROPE_T, LDX, TPN, KVT, RPT, WEARLY, WLATE)
                    (ROPE_T, LDX, TPN, KVT, RPT, WEARLY, WLATE) = sems_cache[0]
                    p.dma("sync", cos_t[:],
                          cos_h[:].rearrange("p a b -> p (a b)"), ROPE_T)
                    rtw = p.dma("sync", sinp_t[:],
                                sinp_h[:].rearrange("p a b -> p (a b)"), ROPE_T)
                    xt = [csb([128, D], F32) for _ in range(2)]
                    xnt = [csb([128, D], BF16) for _ in range(2)]
                    kvt = [csb([128, KVR], BF16) for _ in range(2)]
                    rpt = [csb([128, QR], BF16) for _ in range(2)]
                    rt1 = csb([128, QR], F32)
                    rt2 = csb([128, QR], F32)

                    ew = []
                    for (dsth, srch) in ((w_kv, w_kv_h), (w_q, w_q_h),
                                         (w_kr, w_kr_h), (w_qr, w_qr_h)):
                        ew.append(p.dma("gpsimd", dsth[:],
                                        srch[:].rearrange("p a b -> p (a b)"),
                                        WEARLY, waits=[CONSTS_G]))
                    EARLYW = ew[-1]

                    xt_guard = [None, None]
                    xnt_guard = [None, None]
                    tp_last = None
                    for t in range(NTT):
                        r = t % 2
                        w = [CONSTS] if t < 2 else [xt_guard[r]]
                        ldx = p.dma("sync", xt[r][:],
                                    xh[s, t * 128:(t + 1) * 128, :], LDX[r],
                                    waits=w)
                        eww = [] if xnt_guard[r] is None else [xnt_guard[r]]
                        xw, _ = ln_unit(p, xt[r][:], stt_[r], xnt[r][:],
                                        junk, cst, ldx, extra_write_waits=eww)
                        xt_guard[r] = xw
                        tp_last = p.dma(
                            "sync",
                            bass.AP(xn_fm, t * 128,
                                    [[DC * T, 128], [T, DC], [1, 128]]),
                            xnt[r][:], TPN[t % 2], waits=[xw], transpose=True)
                        xnt_guard[r] = tp_last
                    XN_DONE = (TPN[0], TPN[0].v)
                    XN_DONE2 = (TPN[1], TPN[1].v)

                    kv_guard = [None, None]
                    tp_last = None
                    for t in range(NTT):
                        r = t % 2
                        stb = stt_[r]
                        for c in range(DC):
                            w = [XN_DONE, XN_DONE2, EARLYW] if (c == 0 and t == 0) else []
                            p.add("tensor", lambda e, a=bass.AP(
                                xn_fm, c * T + t * 128, [[DC * T, 128], [1, 128]]),
                                b=bass.AP(w_kv, c * KVR, [[DC * KVR, 128], [1, KVR]]),
                                o=ps[0][:, 0:KVR], s_=(c == 0), sp=(c == DC - 1):
                                e.matmul(o, a, b, start=s_, stop=sp),
                                waits=w, sig=p.TEN if c == DC - 1 else None)
                        mmw = (p.TEN, p.TEN.v)
                        sqw = p.add("scalar", lambda e, i=ps[0][:, 0:KVR],
                                    o=junk[:, 0:KVR], ac=stb[:, 0:1]:
                                    e.activation(o, i, AF.Square, accum_out=ac),
                                    waits=[mmw], sig=p.SCA)
                        rmw = p.add("scalar", lambda e, i=stb[:, 0:1], o=stb[:, 1:2],
                                    b=cst[:, 2:3]:
                                    e.activation(o, i, AF.Sqrt, bias=b,
                                                 scale=1.0 / KVR),
                                    waits=[sqw], sig=p.SCA)
                        p.add("vector", lambda e, i=stb[:, 1:2], o=stb[:, 2:3]:
                              e.reciprocal(o, i), waits=[rmw])
                        p.vsep()
                        w = [] if kv_guard[r] is None else [kv_guard[r]]
                        kvw = p.add("vector", lambda e, i=ps[0][:, 0:KVR],
                                    o=kvt[r][:], rs=stb[:, 2:3], g=g_bc[:]:
                                    e.scalar_tensor_tensor(o, i, rs, g,
                                                           ALU.mult, ALU.mult),
                                    waits=w, sig=p.VEC)
                        p.add("tensor", None, waits=[kvw])
                        tp_last = p.dma(
                            "sync",
                            bass.AP(kvn_fm, t * 128,
                                    [[2 * T, 128], [T, 2], [1, 128]]),
                            kvt[r][:], KVT[t % 2], waits=[kvw], transpose=True)
                        kv_guard[r] = tp_last
                    KVN_DONE = (KVT[0], KVT[0].v)
                    KVN_DONE2 = (KVT[1], KVT[1].v)

                    qc_guard = [None, None]
                    last_ev = None
                    for fc in range(4):
                        for tq in range(4):
                            k = fc * 4 + tq
                            bank = ps[1 + k % 2]
                            for c in range(DC):
                                w = []
                                if c == 0:
                                    if k == 0:
                                        w = [XN_DONE, XN_DONE2]
                                    elif qc_guard[k % 2] is not None:
                                        w = [qc_guard[k % 2]]
                                p.add("tensor", lambda e, a=bass.AP(
                                    w_q, c * QR + fc * 128,
                                    [[DC * QR, 128], [1, 128]]),
                                    b=bass.AP(xn_fm, c * T + tq * 512,
                                              [[DC * T, 128], [1, 512]]),
                                    o=bank[:], s_=(c == 0), sp=(c == DC - 1):
                                    e.matmul(o, a, b, start=s_, stop=sp),
                                    waits=w, sig=p.TEN if c == DC - 1 else None)
                            mmw = (p.TEN, p.TEN.v)
                            last_ev = p.add("vector", lambda e, i=bank[:],
                                            o=bass.AP(qc_fm, fc * T + tq * 512,
                                                      [[4 * T, 128], [1, 512]]):
                                            e.tensor_copy(o, i),
                                            waits=[mmw], sig=p.VEC)
                            qc_guard[k % 2] = last_ev
                    QC_DONE = last_ev

                    def rope_proj(wsb, nk, src, srcn, dst_tok, first_wait):
                        rw = None
                        for t in range(NTT):
                            r = t % 2
                            bank = ps[3 - t % 2]
                            for c in range(nk):
                                w = []
                                if c == 0:
                                    if t == 0:
                                        w = list(first_wait)
                                    else:
                                        w = [(p.VEC, p.VEC.v)]
                                p.add("tensor", lambda e, a=bass.AP(
                                    src, c * T + t * 128,
                                    [[srcn * T, 128], [1, 128]]),
                                    b=bass.AP(wsb, c * QR, [[nk * QR, 128], [1, QR]]),
                                    o=bank[:], s_=(c == 0), sp=(c == nk - 1):
                                    e.matmul(o, a, b, start=s_, stop=sp),
                                    waits=w, sig=p.TEN if c == nk - 1 else None)
                            mmw = (p.TEN, p.TEN.v)
                            cosb = bass.AP(cos_t, t * DR,
                                           [[NTT * DR, 128], [0, H], [1, DR]])
                            sinlo = bass.AP(sinp_t, t * DR,
                                            [[NTT * DR, 128], [0, H], [1, 16]])
                            sinhi = bass.AP(sinp_t, t * DR + 16,
                                            [[NTT * DR, 128], [0, H], [1, 16]])
                            p4 = bank[:].rearrange("p (h u j) -> p h u j",
                                                   h=H, u=2)
                            r4 = rt2[:].rearrange("p (h u j) -> p h u j",
                                                  h=H, u=2)
                            p.add("vector", lambda e,
                                  i=bank[:].rearrange("p (h j) -> p h j", h=H),
                                  cb=cosb,
                                  o=rt1[:].rearrange("p (h j) -> p h j", h=H):
                                  e.tensor_tensor(o, i, cb, ALU.mult), waits=[mmw])
                            p.add("vector", lambda e, i=p4[:, :, 1, :], sb4=sinlo,
                                  o=r4[:, :, 0, :]:
                                  e.tensor_tensor(o, i, sb4, ALU.mult))
                            p.add("vector", lambda e, i=p4[:, :, 0, :], sb4=sinhi,
                                  o=r4[:, :, 1, :]:
                                  e.tensor_tensor(o, i, sb4, ALU.mult))
                            p.vsep()
                            rw = p.add("vector", lambda e, i=rt1[:], j=rt2[:],
                                       o=dst_tok[:, t * 512:(t + 1) * 512]:
                                       e.tensor_tensor(o, i, j, ALU.add),
                                       sig=p.VEC)
                            p.add("tensor", None, waits=[(p.VEC, p.VEC.v)])
                        return rw

                    KR_DONE = rope_proj(w_kr, DC, xn_fm, DC, kr_tok,
                                        [XN_DONE, XN_DONE2, rtw])
                    QR_DONE = rope_proj(w_qr, 4, qc_fm, 4, qr_tok, [QC_DONE])
                    p.fence()
                    if s == 0:
                        tap("xnfm", xn_fm[:], [128, DC * T], BF16)
                        tap("kvnfm", kvn_fm[:], [128, 2 * T], BF16)
                        tap("qcfm", qc_fm[:], [128, 4 * T], BF16)
                        tap("krtok", kr_tok[:], [128, NTT * 512], BF16)
                        tap("qrtok", qr_tok[:], [128, NTT * 512], BF16)
                    p.flush()

                with ExitStack() as dst_:
                    dsb = lambda shape, d: dst_.enter_context(nc.sbuf_tensor(_nm(), shape, d))
                    xq_fm = seqst.enter_context(
                        nc.sbuf_tensor(_nm(), [128, DC * T], BF16))
                    O_fm = dsb([128, DC * T], BF16)
                    r_sb = dsb([128, 4 * T], BF16)

                    with ExitStack() as grp:
                        gsb = lambda shape, d: grp.enter_context(
                            nc.sbuf_tensor(_nm(), shape, d, side="right"))
                        wkd_sl = gsb([128, 2 * 256], BF16)
                        wqd_sl = gsb([128, 4 * 256], BF16)
                        wvd_sl = gsb([128, 2 * 256], BF16)
                        K_fm = gsb([128, 4 * T], BF16)
                        Q_fm = gsb([128, 4 * T], BF16)
                        V_g = gsb([128, NTT * 384], BF16)
                        PT = [gsb([128, 1024], BF16) for _ in range(2)]
                        ktr = [gsb([128, 512], BF16) for _ in range(2)]
                        if s == 0:
                            WDEC = p.dsem("WDEC")
                            KTP = [p.dsem("KTP0"), p.dsem("KTP1")]
                            sems_cache[3] = (WDEC, KTP)
                        (WDEC, KTP) = sems_cache[3]

                        for g in range(NG):
                            gw_ = [(p.TEN, p.TEN.v)]
                            wld = None
                            for (dsl, srch, nk) in ((wkd_sl, w_kd_h, 2),
                                                    (wqd_sl, w_qd_h, 4),
                                                    (wvd_sl, w_vd_h, 2)):
                                for c in range(nk):
                                    wld = p.dma("gpsimd",
                                                dsl[:, c * 256:(c + 1) * 256],
                                                srch[:, c, g * 256:(g + 1) * 256],
                                                WDEC, waits=gw_ + [CONSTS_G])
                                    gw_ = []
                            WSL = wld

                            for (ridx, rope_tok, dsl, nk, dsrc, srcn, dfm) in (
                                    (0, kr_tok, wkd_sl, 2, kvn_fm, 2, K_fm),
                                    (1, qr_tok, wqd_sl, 4, qc_fm, 4, Q_fm)):
                                for t in range(NTT):
                                    r = t % 2
                                    bank = ps[r]
                                    for c in range(nk):
                                        w = []
                                        if c == 0:
                                            w = [(p.VEC, p.VEC.v), WSL]
                                            if t == 0 and ridx == 0 and g == 0:
                                                w += [KVN_DONE, KVN_DONE2]
                                        p.add("tensor", lambda e, a=bass.AP(
                                            dsrc, c * T + t * 128,
                                            [[srcn * T, 128], [1, 128]]),
                                            b=dsl[:, c * 256:(c + 1) * 256],
                                            o=bank[:, 0:256], s_=(c == 0),
                                            sp=(c == nk - 1):
                                            e.matmul(o, a, b, start=s_, stop=sp),
                                            waits=w,
                                            sig=p.TEN if c == nk - 1 else None)
                                    mmw = (p.TEN, p.TEN.v)
                                    w = [mmw, (KTP[r], KTP[r].v)]
                                    p.add("vector", lambda e, i=bank[:, 0:256],
                                          o=bass.AP(ktr[r], 0,
                                                    [[512, 128], [128, 4], [1, 64]]):
                                          e.tensor_copy(o, i), waits=w)
                                    cw = p.add("vector", lambda e, i=bass.AP(
                                        rope_tok, t * 512 + g * 128,
                                        [[NTT * 512, 128], [1, 128]]),
                                        o=bass.AP(ktr[r], 64,
                                                  [[512, 128], [128, 4], [1, 32]]):
                                        e.tensor_copy(o, i), sig=p.VEC)
                                    p.dma("sync", bass.AP(
                                        dfm, t * 128,
                                        [[4 * T, 128], [T, 4], [1, 128]]),
                                        ktr[r][:], KTP[r], waits=[cw],
                                        transpose=True)
                            KQ_DONE = [(KTP[0], KTP[0].v), (KTP[1], KTP[1].v)]

                            if g == 0:
                                for pr in range(2):
                                    p.add("vector", lambda e, o=bass.AP(
                                        V_g, pr * 192 + 64,
                                        [[NTT * 384, 128], [384, NTT], [1, 64]]):
                                        e.memset(o, 0.0))
                                    p.add("vector", lambda e, o=bass.AP(
                                        V_g, pr * 192 + 64 + 32 * pr,
                                        [[NTT * 384, 128], [384, NTT], [1, 1]]):
                                        e.memset(o, 1.0), sig=p.VEC)
                            STRIPS = (p.VEC, p.VEC.v)
                            for t in range(NTT):
                                bank = ps[t % 2]
                                for c in range(2):
                                    w = []
                                    if c == 0:
                                        w = [(p.VEC, p.VEC.v), WSL]
                                        if t == 0:
                                            w += [STRIPS, KVN_DONE, KVN_DONE2]
                                    p.add("tensor", lambda e, a=bass.AP(
                                        kvn_fm, c * T + t * 128,
                                        [[2 * T, 128], [1, 128]]),
                                        b=wvd_sl[:, c * 256:(c + 1) * 256],
                                        o=bank[:, 0:256], s_=(c == 0), sp=(c == 1):
                                        e.matmul(o, a, b, start=s_, stop=sp),
                                        waits=w, sig=p.TEN if c == 1 else None)
                                mmw = (p.TEN, p.TEN.v)
                                p.add("vector", lambda e,
                                      i=bank[:, 0:256].rearrange(
                                          "p (pr sd dh) -> p pr sd dh", pr=2, sd=2),
                                      o=bass.AP(V_g, t * 384,
                                                [[NTT * 384, 128], [192, 2],
                                                 [128, 2], [1, 64]]):
                                      e.tensor_copy(o, i), waits=[mmw], sig=p.VEC)
                            VG_DONE = (p.VEC, p.VEC.v)
                            p.fence()
                            if s == 0 and g == 0:
                                tap("kfm", K_fm[:], [128, 4 * T], BF16)
                                tap("qfm", Q_fm[:], [128, 4 * T], BF16)
                                tap("vg", V_g[:], [128, NTT * 384], BF16)

                            for hh in range(GH):
                                h = 4 * g + hh
                                vwin = (hh // 2) * 192 + (64 if (hh % 2) else 0)
                                prow = (64 + 32 * (hh // 2)) if hh % 2 == 0 \
                                    else (32 * (hh // 2))
                                for jb in range(4):
                                    ob = ps[2 + jb % 2]
                                    nk = 4 * jb + 4
                                    pv_bank_w = (p.VEC, p.VEC.v)
                                    half_guard = {}
                                    for ii in range(0, 4 * jb, 2):
                                        pb = (ii // 2) % 2
                                        sp_ = psS[pb]
                                        for d in range(2):
                                            i = ii + d
                                            hk = pb * 2 + d
                                            w = []
                                            if hk in half_guard:
                                                w = [half_guard[hk]]
                                            if ii == 0 and d == 0:
                                                w = w + KQ_DONE
                                            p.add("tensor", lambda e, a=bass.AP(
                                                K_fm, hh * T + i * 128,
                                                [[4 * T, 96], [1, 128]]),
                                                b=bass.AP(Q_fm, hh * T + jb * 512,
                                                          [[4 * T, 96], [1, 512]]),
                                                o=sp_[:, d * 512:(d + 1) * 512]:
                                                e.matmul(o, a, b, start=True,
                                                         stop=True),
                                                waits=w,
                                                sig=p.TEN if d == 1 else None)
                                        qkw = (p.TEN, p.TEN.v)
                                        slot = PT[pb]
                                        ew_ = p.add("scalar", lambda e,
                                                    i_=sp_[:, 0:1024],
                                                    o=slot[:, 0:1024],
                                                    b=cst[:, 0:1]:
                                                    e.activation(o, i_, AF.Exp,
                                                                 bias=b,
                                                                 scale=SCALE),
                                                    waits=[qkw], sig=p.SCA)
                                        half_guard[pb * 2] = ew_
                                        half_guard[pb * 2 + 1] = ew_
                                        for d in range(2):
                                            i = ii + d
                                            w = [ew_]
                                            if i == 0:
                                                w += [pv_bank_w, VG_DONE]
                                            p.add("tensor", lambda e, a=bass.AP(
                                                V_g, i * 384 + vwin,
                                                [[NTT * 384, 128], [1, 128]]),
                                                b=slot[:, d * 512:(d + 1) * 512],
                                                o=ob[:], s_=(i == 0):
                                                e.matmul(o, a, b, start=s_,
                                                         stop=False),
                                                waits=w)
                                    for m in range(4):
                                        i = 4 * jb + m
                                        pb = (i // 2) % 2
                                        dhalf = i % 2
                                        hk = pb * 2 + dhalf
                                        sp_h = psS[pb][:, dhalf * 512:
                                                       dhalf * 512 + 512]
                                        lo = 128 * m
                                        w = []
                                        if hk in half_guard:
                                            w = [half_guard[hk]]
                                        if i == 0:
                                            w = w + KQ_DONE
                                        qkw = p.add("tensor", lambda e, a=bass.AP(
                                            K_fm, hh * T + i * 128,
                                            [[4 * T, 96], [1, 128]]),
                                            b=bass.AP(Q_fm,
                                                      hh * T + jb * 512 + lo,
                                                      [[4 * T, 96], [1, 512 - lo]]),
                                            o=sp_h[:, lo:512]:
                                            e.matmul(o, a, b, start=True,
                                                     stop=True),
                                            waits=w, sig=p.TEN)
                                        slot = PT[pb]
                                        off = dhalf * 512
                                        ew_ = p.add("scalar", lambda e,
                                                    i_=sp_h[:, lo:512],
                                                    o=slot[:, off + lo:off + 512],
                                                    b=cst[:, 0:1]:
                                                    e.activation(o, i_, AF.Exp,
                                                                 bias=b,
                                                                 scale=SCALE),
                                                    waits=[qkw], sig=p.SCA)
                                        half_guard[hk] = ew_
                                        dep = p.add("vector", lambda e,
                                                    o=slot[:, off + lo:
                                                           off + lo + 128],
                                                    tr=tri[:]:
                                                    e.tensor_tensor(o, o, tr,
                                                                    ALU.mult),
                                                    waits=[ew_], sig=p.VEC)
                                        half_guard[hk] = dep
                                        w = [dep]
                                        if i == 0:
                                            w += [pv_bank_w, VG_DONE]
                                        p.add("tensor", lambda e, a=bass.AP(
                                            V_g, i * 384 + vwin,
                                            [[NTT * 384, 128], [1, 128]]),
                                            b=slot[:, off + lo:off + 512],
                                            o=ob[:, lo:512],
                                            s_=(i == 0), sp=(i == nk - 1):
                                            e.matmul(o, a, b, start=s_, stop=sp),
                                            waits=w,
                                            sig=p.TEN if i == nk - 1 else None)
                                    pvw = (p.TEN, p.TEN.v)
                                    orow = 0 if h % 2 == 0 else 64
                                    p.add("vector", lambda e,
                                        i=ob[orow:orow + 64, :],
                                        o=bass.AP(O_fm,
                                                  orow * DC * T + (h // 2) * T
                                                  + jb * 512,
                                                  [[DC * T, 64], [1, 512]]):
                                        e.tensor_copy(o, i), waits=[pvw])
                                    p.add("vector", lambda e,
                                        i=ob[prow:prow + 1, :],
                                        o=bass.AP(r_sb,
                                                  prow * 4 * T + g * T + jb * 512,
                                                  [[4 * T, 1], [1, 512]]):
                                        e.tensor_copy(o, i), sig=p.VEC)
                            p.fence()
                        p.flush()

                    with ExitStack() as eph:
                        esb = lambda shape, d: eph.enter_context(nc.sbuf_tensor(_nm(), shape, d))
                        w_o = esb([128, DC * D], BF16)
                        WOL = p.dma("gpsimd", w_o[:],
                                    w_o_h[:].rearrange("p a b -> p (a b)"),
                                    WLATE, waits=[CONSTS_G])
                        R_rep = esb([128, T], F32)
                        xt2 = [esb([128, D], F32) for _ in range(2)]
                        x2t = [esb([128, D], F32) for _ in range(2)]
                        x2n = esb([128, D], F32)
                        xqt = [esb([128, D], BF16) for _ in range(2)]
                        qbE = QBufs(eph, nc, 512)
                        if s == 0:
                            tap("ofm_pre", O_fm[:], [128, DC * T], BF16)
                            tap("rsb", r_sb[:], [128, 4 * T], BF16)
                            RS1 = p.dsem("RS1")
                            RS2 = p.dsem("RS2")
                            RS3 = p.dsem("RS3")
                            RREP = [p.dsem("RREP0"), p.dsem("RREP1")]
                            X2W = [p.dsem("X2W0"), p.dsem("X2W1")]
                            XQT = [p.dsem("XQT0"), p.dsem("XQT1")]
                            sems_cache[1] = (RS1, RS2, RS3, RREP, X2W, XQT)
                        (RS1, RS2, RS3, RREP, X2W, XQT) = sems_cache[1]
                        rcp = esb([128, 256], BF16)
                        rcp2 = esb([128, 256], F32)

                        w2 = None
                        for g_ in range(NG):
                            w2 = p.dma("sync",
                                       bass.AP(r_s, (s * H + 4 * g_) * T,
                                               [[T, 4], [1, T]]),
                                       bass.AP(r_sb, g_ * T,
                                               [[32 * 4 * T, 4], [1, T]]), RS1)
                        ldr = p.dma("sync", rcp[:],
                                    bass.AP(r_s, s * H * T, [[256, 128], [1, 256]]),
                                    RS2, waits=[w2])
                        rcw = p.add("vector", lambda e, i=rcp[:], o=rcp2[:]:
                                    e.reciprocal(o, i), waits=[ldr], sig=p.VEC)
                        str_ = p.dma("sync",
                                     bass.AP(r_s2, s * H * T, [[256, 128], [1, 256]]),
                                     rcp2[:], RS3, waits=[rcw])
                        ng = [None, None]
                        last_norm = None
                        for h in range(H):
                            half = (h % 2) * 64
                            w = [str_]
                            if ng[h % 2] is not None:
                                w.append(ng[h % 2])
                            rep = p.dma("sync",
                                        bass.AP(R_rep, half * T, [[T, 64], [1, T]]),
                                        bass.AP(r_s2,
                                                (s * H + 4 * (h // 4)
                                                 + SLOT[h % 4]) * T,
                                                [[0, 64], [1, T]]),
                                        RREP[h % 2], waits=w)
                            nw = p.add("vector", lambda e, o=bass.AP(
                                O_fm, half * DC * T + (h // 2) * T,
                                [[DC * T, 64], [1, T]]),
                                rp=bass.AP(R_rep, half * T, [[T, 64], [1, T]]):
                                e.tensor_tensor(o, o, rp, ALU.mult),
                                waits=[rep], sig=p.VEC)
                            ng[h % 2] = nw
                            last_norm = nw
                        if s == 0:
                            p.fence()
                            tap("ofm_post", O_fm[:], [128, DC * T], BF16)

                        xt_guard = [None, None]
                        x2_guard = [None, None]
                        xq_guard = [None, None]
                        tp_last = None
                        for t in range(NTT):
                            r = t % 2
                            w = [] if xt_guard[r] is None else [xt_guard[r]]
                            ldx = p.dma("sync", xt2[r][:],
                                        xh[s, t * 128:(t + 1) * 128, :], LDX[r],
                                        waits=w)
                            evs = None
                            for f2 in range(2):
                                bank = ps[f2]
                                for c in range(DC):
                                    w = []
                                    if c == 0:
                                        w = [(p.VEC, p.VEC.v)]
                                        if t == 0 and f2 == 0:
                                            w.append(last_norm)
                                            w.append(WOL)
                                    p.add("tensor", lambda e, a=bass.AP(
                                        O_fm, c * T + t * 128,
                                        [[DC * T, 128], [1, 128]]),
                                        b=bass.AP(w_o, c * D + f2 * 512,
                                                  [[DC * D, 128], [1, 512]]),
                                        o=bank[:], s_=(c == 0), sp=(c == DC - 1):
                                        e.matmul(o, a, b, start=s_, stop=sp),
                                        waits=w,
                                        sig=p.TEN if c == DC - 1 else None)
                                mmw = (p.TEN, p.TEN.v)
                                w = [mmw, ldx]
                                if x2_guard[r] is not None and f2 == 0:
                                    w += x2_guard[r]
                                evs = p.add("vector", lambda e, i=bank[:],
                                            j=xt2[r][:, f2 * 512:(f2 + 1) * 512],
                                            o=x2t[r][:, f2 * 512:(f2 + 1) * 512]:
                                            e.tensor_tensor(o, i, j, ALU.add),
                                            waits=w, sig=p.VEC)
                            xt_guard[r] = evs
                            x2w = p.dma("sync", x2_s[s, t * 128:(t + 1) * 128, :],
                                        x2t[r][:], X2W[r], waits=[evs])
                            lw_, sqw = ln_unit(p, x2t[r][:], stt_[r], x2n[:],
                                               junk, cst, evs)
                            x2_guard[r] = [lw_, x2w, sqw]
                            qw = None
                            for hf in range(2):
                                pw = [lw_]
                                if hf == 0 and xq_guard[r] is not None:
                                    pw.append(xq_guard[r])
                                qw = quant_unit(p, qbE,
                                                x2n[:, hf * 512:(hf + 1) * 512],
                                                32, xqt[r][:, hf * 512:(hf + 1) * 512],
                                                pre_waits=pw)
                            tp_last = p.dma(
                                "sync",
                                bass.AP(xq_fm, t * 128,
                                        [[DC * T, 128], [T, DC], [1, 128]]),
                                xqt[r][:], XQT[t % 2], waits=[qw], transpose=True)
                            xq_guard[r] = tp_last
                        XQ_DONE = (XQT[0], XQT[0].v)
                        XQ_DONE2 = (XQT[1], XQT[1].v)
                        p.fence()
                        if s == 0:
                            tap("xqfm", xq_fm[:], [128, DC * T], BF16)
                        p.flush()

            with ExitStack() as ffn:
                fsb = lambda shape, d: ffn.enter_context(nc.sbuf_tensor(_nm(), shape, d))
                wgq = fsb([128, DC * FF], BF16)
                wuq = fsb([128, DC * FF], BF16)
                h_tok = fsb([128, 4 * FF], BF16)
                hqT = fsb([128, FFC * 512], BF16)
                g_sb = [fsb([128, 512], BF16) for _ in range(2)]
                wdq = [fsb([128, D], BF16) for _ in range(4)]
                x2r = [fsb([128, D], F32) for _ in range(2)]
                yob = [fsb([128, 512], F32) for _ in range(2)]
                qbF = QBufs(ffn, nc, 512)
                if s == 0:
                    WGL = p.dsem("WGL")
                    WDQ = [p.dsem(f"WDQ{i}") for i in range(4)]
                    HQT = p.dsem("HQT")
                    YST = [p.dsem("YST0"), p.dsem("YST1")]
                    sems_cache[2] = (WGL, WDQ, HQT, YST)
                (WGL, WDQ, HQT, YST) = sems_cache[2]

                ldg = p.dma("sync", wgq[:],
                            bass.AP(wgq_s, 0, [[FF, 128], [128 * FF, DC], [1, FF]]),
                            WGL)
                ldu = p.dma("sync", wuq[:],
                            bass.AP(wuq_s, 0, [[FF, 128], [128 * FF, DC], [1, FF]]),
                            WGL)
                NS = [512] * 5 + [256]
                x2r_guard = [None, None]
                wd_guard = [None] * 4
                prev_down = None
                sca_tok = [None, None]
                vec_tok = [None, None]
                yw_tok = [None] * 8
                for ts4 in range(4):
                    for nsi in range(6):
                        nw_ = NS[nsi]
                        n0 = nsi * 512
                        for tt in range(4):
                            t = ts4 * 4 + tt
                            gb = ps[tt % 2]
                            ub = ps[2 + tt % 2]
                            for c in range(DC):
                                w = []
                                if c == 0:
                                    w = [x for x in (sca_tok[tt % 2],
                                                     yw_tok[tt % 2])
                                         if x is not None]
                                    if ts4 == 0 and nsi == 0 and tt == 0:
                                        w += [XQ_DONE, XQ_DONE2, ldg, ldu]
                                p.add("tensor", lambda e, a=bass.AP(
                                    xq_fm, c * T + t * 128,
                                    [[DC * T, 128], [1, 128]]),
                                    b=bass.AP(wgq, c * FF + n0,
                                              [[DC * FF, 128], [1, nw_]]),
                                    o=gb[:, 0:nw_], s_=(c == 0), sp=(c == DC - 1):
                                    e.matmul(o, a, b, start=s_, stop=sp),
                                    waits=w, sig=p.TEN if c == DC - 1 else None)
                            gmw = (p.TEN, p.TEN.v)
                            for c in range(DC):
                                w = []
                                if c == 0:
                                    w = [x for x in (vec_tok[tt % 2],
                                                     yw_tok[2 + tt % 2])
                                         if x is not None]
                                p.add("tensor", lambda e, a=bass.AP(
                                    xq_fm, c * T + t * 128,
                                    [[DC * T, 128], [1, 128]]),
                                    b=bass.AP(wuq, c * FF + n0,
                                              [[DC * FF, 128], [1, nw_]]),
                                    o=ub[:, 0:nw_], s_=(c == 0), sp=(c == DC - 1):
                                    e.matmul(o, a, b, start=s_, stop=sp),
                                    waits=w, sig=p.TEN if c == DC - 1 else None)
                            umw = (p.TEN, p.TEN.v)
                            sl_w = [gmw]
                            if vec_tok[tt % 2] is not None:
                                sl_w.append(vec_tok[tt % 2])
                            sw = p.add("scalar", lambda e, i=gb[:, 0:nw_],
                                       o=g_sb[tt % 2][:, 0:nw_]:
                                       e.activation(o, i, AF.Silu),
                                       waits=sl_w, sig=p.SCA)
                            sca_tok[tt % 2] = sw
                            hm_w = [sw, umw]
                            if nsi == 0:
                                hm_w.append((HQT, HQT.v))
                            vec_tok[tt % 2] = p.add(
                                "vector", lambda e, i=g_sb[tt % 2][:, 0:nw_],
                                j=ub[:, 0:nw_],
                                o=h_tok[:, tt * FF + n0: tt * FF + n0 + nw_]:
                                e.tensor_tensor(o, i, j, ALU.mult),
                                waits=hm_w, sig=p.VEC)
                    H_DONE = (p.VEC, p.VEC.v)
                    hq_last = None
                    QCH = [512] * 5 + [256]
                    for tt in range(4):
                        qw = None
                        q0 = 0
                        for qt in range(6):
                            pw = []
                            if tt == 0 and qt == 0:
                                pw = [H_DONE]
                            qw = quant_unit(
                                p, qbF,
                                h_tok[:, tt * FF + q0: tt * FF + q0 + QCH[qt]],
                                QCH[qt] // 16,
                                h_tok[:, tt * FF + q0: tt * FF + q0 + QCH[qt]],
                                pre_waits=pw)
                            q0 += QCH[qt]
                        tq_w = [qw]
                        if prev_down is not None:
                            tq_w.append(prev_down)
                        hq_last = p.dma(
                            "sync",
                            bass.AP(hqT, tt * 128,
                                    [[FFC * 512, 128], [512, FFC], [1, 128]]),
                            h_tok[:, tt * FF: tt * FF + FF], HQT,
                            waits=tq_w, transpose=True)
                    HQ_DONE = hq_last
                    for nci in range(FFC):
                        r4 = nci % 4
                        w = []
                        if wd_guard[r4] is not None:
                            w = [wd_guard[r4]]
                        ldw = p.dma("sync", wdq[r4][:],
                                    wdq_s[nci * 128:(nci + 1) * 128, :], WDQ[r4],
                                    waits=w)
                        for tt in range(4):
                            for f2 in range(2):
                                bank = ps[tt * 2 + f2]
                                w = []
                                if tt == 0 and f2 == 0:
                                    w = [ldw]
                                    if nci == 0:
                                        w += [HQ_DONE, (p.VEC, p.VEC.v)]
                                p.add("tensor", lambda e, a=bass.AP(
                                    hqT, nci * 512 + tt * 128,
                                    [[FFC * 512, 128], [1, 128]]),
                                    b=wdq[r4][:, f2 * 512:(f2 + 1) * 512],
                                    o=bank[:], s_=(nci == 0), sp=(nci == FFC - 1):
                                    e.matmul(o, a, b, start=s_, stop=sp),
                                    waits=w,
                                    sig=p.TEN if (tt == 3 and f2 == 1) else None)
                        wd_guard[r4] = (p.TEN, p.TEN.v)
                    DOWN_DONE = (p.TEN, p.TEN.v)
                    prev_down = DOWN_DONE
                    yst = None
                    for tt in range(4):
                        t = ts4 * 4 + tt
                        r = tt % 2
                        w = [] if x2r_guard[r] is None else [x2r_guard[r]]
                        ldx2 = p.dma("sync", x2r[r][:],
                                     x2_s[s, t * 128:(t + 1) * 128, :], LDX[r],
                                     waits=w)
                        for f2 in range(2):
                            bank = ps[tt * 2 + f2]
                            yw = p.add("vector", lambda e, i=bank[:],
                                       j=x2r[r][:, f2 * 512:(f2 + 1) * 512],
                                       o=yob[f2][:]:
                                       e.tensor_tensor(o, i, j, ALU.add),
                                       waits=[DOWN_DONE, ldx2,
                                              (YST[f2], YST[f2].v)], sig=p.VEC)
                            x2r_guard[r] = yw
                            yw_tok[tt * 2 + f2] = yw
                            yst = p.dma(
                                "sync",
                                yh[s, t * 128:(t + 1) * 128,
                                   f2 * 512:(f2 + 1) * 512],
                                yob[f2][:], YST[f2], waits=[yw])
                    if ts4 == 3:
                        p.fence()
                p.flush()

    p.ops["sync"].append(([(s_.h, s_.v) for s_ in p.dma_sems], None, [], False))
    p.flush()
    st.close()
    return nc


def _rope_tables():
    inv = 1.0 / 10000.0 ** (np.arange(0, DR, 2, dtype=np.float32) / DR)
    t = np.arange(T, dtype=np.float32)
    freqs = np.outer(t, inv)
    emb = np.concatenate([freqs, freqs], -1)
    cos = np.cos(emb).astype(np.float32)
    sinp = np.sin(emb).astype(np.float32)
    sinp[:, :DR // 2] *= -1.0
    cos_t = np.ascontiguousarray(cos.reshape(NTT, 128, DR).transpose(1, 0, 2))
    sinp_t = np.ascontiguousarray(sinp.reshape(NTT, 128, DR).transpose(1, 0, 2))
    return cos_t, sinp_t


def _rope_perm():
    perm = np.zeros(QR, dtype=np.int64)
    for rc in range(4):
        for m in range(4):
            h = 4 * rc + m
            sl = SLOT[m]
            perm[rc * 128 + sl * 32: rc * 128 + sl * 32 + 32] = \
                np.arange(h * 32, h * 32 + 32)
    return perm


def _pack_km(w):
    K, F = w.shape
    return np.ascontiguousarray(w.reshape(K // 128, 128, F).transpose(1, 0, 2))


def _pack_nk(w):
    N, K = w.shape
    return np.ascontiguousarray(w.reshape(N // 128, 128, K).transpose(1, 0, 2))


_CACHE = {}


def kernel(L0_attn_x, L0_attn_W_kv, L0_attn_g_kv, L0_attn_W_kdec, L0_attn_W_vdec,
           L0_attn_W_q, L0_attn_W_qdec, L0_attn_W_krope, L0_attn_W_qrope,
           L0_attn_W_o, L0_ffn_w_gate, L0_ffn_w_up, L0_ffn_w_down):
    x = np.ascontiguousarray(np.asarray(L0_attn_x, dtype=np.float32))
    perm = _rope_perm()
    cos_t, sinp_t = _rope_tables()
    tri = np.triu(np.ones((128, 128), dtype=np.float32))
    g_bc = np.ascontiguousarray(np.broadcast_to(
        np.asarray(L0_attn_g_kv, np.float32)[None, :], (128, KVR)))
    cst = np.zeros((128, 8), dtype=np.float32)
    cst[:, 0] = CSHIFT
    cst[:, 1] = 1e-5
    cst[:, 2] = 1e-6

    wmaps = {
        "w_kv": _pack_km(np.asarray(L0_attn_W_kv, np.float32)),
        "w_q": _pack_km(np.asarray(L0_attn_W_q, np.float32)),
        "w_kr": _pack_km(np.asarray(L0_attn_W_krope, np.float32)),
        "w_qr": _pack_km(np.asarray(L0_attn_W_qrope, np.float32)),
        "w_kd": _pack_km(np.asarray(L0_attn_W_kdec, np.float32)),
        "w_qd": _pack_km(np.asarray(L0_attn_W_qdec, np.float32)),
        "w_vd": _pack_km(np.asarray(L0_attn_W_vdec, np.float32)),
        "w_o": _pack_km(np.asarray(L0_attn_W_o, np.float32)),
        "wg": _pack_nk(np.asarray(L0_ffn_w_gate, np.float32)),
        "wu": _pack_nk(np.asarray(L0_ffn_w_up, np.float32)),
        "wd": _pack_nk(np.asarray(L0_ffn_w_down, np.float32)),
        "cos_t": cos_t, "sinp_t": sinp_t, "tri": tri, "g_bc": g_bc, "cst": cst,
    }

    if "nc" not in _CACHE:
        _CACHE["nc"] = build_program()
    nc = _CACHE["nc"]

    in_maps = []
    for c in range(NCORES):
        m = dict(wmaps)
        m["x"] = np.ascontiguousarray(x[2 * c: 2 * c + 2])
        in_maps.append(m)
    res = run_bass_kernel_spmd(nc, in_maps, core_ids=list(range(NCORES)))
    out = np.concatenate([np.asarray(res.results[c]["y"]) for c in range(NCORES)],
                         axis=0)
    return out.astype(np.float32)


# version 14
# speedup vs baseline: 1.1605x; 1.0173x over previous
import math
import numpy as np
import concourse.bass as bass
import concourse.mybir as mybir
from concourse.bass_utils import run_bass_kernel_spmd

F32 = mybir.dt.float32
BF16 = mybir.dt.bfloat16
AF = mybir.ActivationFunctionType
ALU = mybir.AluOpType
AX = mybir.AxisListType

NCORES = 8
SEQS = 2
T = 2048
NTT = 16
D = 1024
DC = 8
H = 16
DR = 32
KVR = 256
QR = 512
FF = 2816
FFC = 22
NG = 4
GH = 4
SCALE = 1.0 / math.sqrt(96.0)
CSHIFT = -12.0
MH = 6291456.0
M1 = 12582912.0
M2 = 25165824.0
SLOT = [2, 0, 3, 1]
DEBUG = False

ENGS = ("sync", "gpsimd", "tensor", "vector", "scalar")

_NAME_N = [0]


def _nm(pfx="t"):
    _NAME_N[0] += 1
    return f"{pfx}{_NAME_N[0]}"



class _Sem:
    __slots__ = ("h", "v")

    def __init__(self, h):
        self.h = h
        self.v = 0


class Prog:
    def __init__(self, nc, stack):
        self.nc = nc
        self.ops = {e: [] for e in ENGS}
        self.last_inst = {e: None for e in ENGS}
        self.TEN = _Sem(stack.enter_context(nc.semaphore("sTEN")))
        self.VEC = _Sem(stack.enter_context(nc.semaphore("sVEC")))
        self.SCA = _Sem(stack.enter_context(nc.semaphore("sSCA")))
        self.GPS = _Sem(stack.enter_context(nc.semaphore("sGPS")))
        self.FEN = _Sem(stack.enter_context(nc.semaphore("sFEN")))
        self._stack = stack
        self.dma_sems = []
        self._fence_marks = {}
        self.esem = {"tensor": self.TEN, "vector": self.VEC,
                     "scalar": self.SCA, "gpsimd": self.GPS}

    def dsem(self, name):
        s = _Sem(self._stack.enter_context(self.nc.semaphore(name)))
        self.dma_sems.append(s)
        return s

    def add(self, eng, emit, waits=(), sig=None, dma=None):
        w = [(s.h, v) for (s, v) in waits if s is not None]
        snap = None
        incs = []
        if sig is not None:
            sig.v += 1
            incs.append((sig.h, 1))
            snap = (sig, sig.v)
        if dma is not None:
            dma.v += 16
            incs.append((dma.h, 16))
            snap = (dma, dma.v)
        self.ops[eng].append((w, emit, incs, dma is not None))
        return snap

    def dma(self, eng, out_ap, in_ap, sem, waits=(), transpose=False):

        def emit(e, o=out_ap, i=in_ap, tp=transpose):
            if tp:
                return e.dma_start_transpose(o, i)
            return e.dma_start(o, i)

        return self.add(eng, emit, waits=waits, dma=sem)

    def _ensure_signaled(self, e):
        sem = self.esem[e]
        for op in reversed(self.ops[e]):
            w, emit, incs, isdma = op
            if emit is None or isdma:
                continue
            for (h, amt) in incs:
                if h is sem.h:
                    return
            sem.v += 1
            incs.append((sem.h, 1))
            return

    def vsep(self):
        self.ops["vector"].append(([], lambda e: e.drain(), [], False))

    def fence(self):
        for e in ("tensor", "vector", "scalar", "gpsimd"):
            self._ensure_signaled(e)
        waits = [(self.TEN.h, self.TEN.v), (self.VEC.h, self.VEC.v),
                 (self.SCA.h, self.SCA.v), (self.GPS.h, self.GPS.v)]
        for s in self.dma_sems:
            if self._fence_marks.get(id(s)) != s.v:
                waits.append((s.h, s.v))
                self._fence_marks[id(s)] = s.v
        for e in ENGS:
            self.ops[e].append((list(waits), None, [], False))

    def flush(self):
        with self.nc.Block() as block:
            for engname in ENGS:
                lst = self.ops[engname]
                if not lst:
                    continue

                def body(eng, mylst=lst, en=engname):
                    for (w, emit, incs, isdma) in mylst:
                        for (h, v) in w:
                            eng.wait_ge(h, v)
                        inst = emit(eng) if emit is not None else None
                        if inst is not None:
                            self.last_inst[en] = inst
                        if incs:
                            tgt = inst if inst is not None else self.last_inst[en]
                            if tgt is None:
                                for (h, amt) in incs:
                                    eng.sem_inc(h, amt)
                            else:
                                for (h, amt) in incs:
                                    tgt.then_inc(h, amt)

                getattr(block, engname)(body)
        self.ops = {e: [] for e in ENGS}


class QBufs:

    def __init__(self, stack, nc, qw):
        sb = lambda shape, d: stack.enter_context(nc.sbuf_tensor(_nm(), shape, d))
        self.qw = qw
        nb = qw // 16
        self.Q1 = sb([128, qw], F32)
        self.Q2 = sb([128, qw], F32)
        self.Q3 = sb([128, qw], F32)
        self.Q4 = sb([128, qw], F32)
        self.Q5 = sb([128, qw], F32)
        self.Qs = sb([128, qw], BF16)
        self.sm = sb([128, 3 * nb], F32)
        self.nb = nb


def quant_unit(p, qb, in_ap, nb, out_ap, pre_waits=()):
    n = nb * 16
    NB = qb.nb
    bm = qb.sm[:, 0:nb]
    sS = qb.sm[:, NB:NB + nb]
    sinv = qb.sm[:, 2 * NB:2 * NB + nb]
    in3 = in_ap.rearrange("p (b k) -> p b k", k=16)
    p.add("vector", lambda e, i=in3, o=bm.rearrange("p (b o) -> p b o", o=1):
          e.tensor_reduce(o, i, AX.X, ALU.max, apply_absolute_value=True),
          waits=pre_waits)
    p.vsep()
    p.add("vector", lambda e, i=bm: e.tensor_single_scalar(i, i, 6e-12, ALU.max))
    p.vsep()
    p.add("vector", lambda e, i=bm, o=sS:
          e.tensor_single_scalar(o, i, 1.0 / 6.0, ALU.mult))
    p.vsep()
    p.add("vector", lambda e, i=sS, o=sinv: e.reciprocal(o, i))
    p.vsep()
    sinv_b = bass.AP(qb.sm, 2 * NB, [[3 * NB, 128], [1, nb], [0, 16]])
    s_b = bass.AP(qb.sm, NB, [[3 * NB, 128], [1, nb], [0, 16]])
    z = qb.Q1[:, 0:n]
    a = qb.Q2[:, 0:n]
    zw = p.add("vector", lambda e, i=in3, o=z.rearrange("p (b k) -> p b k", k=16),
               sv=sinv_b: e.tensor_tensor(o, i, sv, ALU.mult), sig=p.VEC)
    aw = p.add("scalar", lambda e, i=z, o=a: e.activation(o, i, AF.Abs),
               waits=[zw], sig=p.SCA)
    gw = p.add("scalar", lambda e, i=z, o=qb.Qs[:, 0:n]:
               e.activation(o, i, AF.Sign), sig=p.SCA)
    p.add("vector", lambda e, i=a, o=qb.Q5[:, 0:n]:
          e.tensor_scalar(o, i, MH, 2.0 + MH, ALU.add, ALU.min), waits=[aw])
    p.add("vector", lambda e, i=a, o=qb.Q3[:, 0:n]:
          e.tensor_scalar(o, i, M1, 2.0 + M1, ALU.add, ALU.max))
    p.add("vector", lambda e, i=qb.Q3[:, 0:n]:
          e.tensor_scalar(i, i, 4.0 + M1, M1 + 2.0, ALU.min, ALU.subtract))
    p.add("vector", lambda e, i=a, o=qb.Q4[:, 0:n]:
          e.tensor_scalar(o, i, M2, 4.0 + M2, ALU.add, ALU.max))
    p.add("vector", lambda e, i=qb.Q4[:, 0:n]:
          e.tensor_scalar(i, i, 6.0 + M2, M2 + 4.0, ALU.min, ALU.subtract))
    p.add("vector", lambda e, i=qb.Q5[:, 0:n], j=qb.Q3[:, 0:n], o=z:
          e.scalar_tensor_tensor(o, i, MH, j, ALU.subtract, ALU.add),
          waits=[gw])
    p.add("vector", lambda e, i=z, j=qb.Q4[:, 0:n]:
          e.tensor_tensor(i, i, j, ALU.add))
    p.add("vector", lambda e, i=z, j=qb.Qs[:, 0:n]:
          e.tensor_tensor(i, i, j, ALU.mult))
    w = p.add("vector",
              lambda e, i=z.rearrange("p (b k) -> p b k", k=16),
              o=out_ap.rearrange("p (b k) -> p b k", k=16), sv=s_b:
              e.tensor_tensor(o, i, sv, ALU.mult), sig=p.VEC)
    return w


def ln_unit(p, src_ap, stb, out_ap, junk, cst, ld_wait, extra_write_waits=()):
    sqw = p.add("scalar", lambda e, i=src_ap, o=junk[:, 0:D], ac=stb[:, 0:1]:
                e.activation(o, i, AF.Square, accum_out=ac),
                waits=[ld_wait], sig=p.SCA)
    p.add("vector", lambda e, i=src_ap, o=stb[:, 1:2]:
          e.reduce_sum(o, i, AX.X), waits=[ld_wait])
    p.vsep()
    p.add("vector", lambda e, i=stb[:, 1:2], o=stb[:, 2:3]:
          e.tensor_single_scalar(o, i, 1.0 / D, ALU.mult))
    p.vsep()
    p.add("vector", lambda e, i=stb[:, 2:3], o=stb[:, 3:4]:
          e.tensor_tensor(o, i, i, ALU.mult))
    p.vsep()
    vw = p.add("vector", lambda e, i=stb[:, 0:1], j=stb[:, 3:4], o=stb[:, 4:5]:
               e.scalar_tensor_tensor(o, i, 1.0 / D, j, ALU.mult, ALU.subtract),
               waits=[sqw], sig=p.VEC)
    sw = p.add("scalar", lambda e, i=stb[:, 4:5], o=stb[:, 5:6], b=cst[:, 1:2]:
               e.activation(o, i, AF.Sqrt, bias=b), waits=[vw], sig=p.SCA)
    p.add("vector", lambda e, i=stb[:, 5:6], o=stb[:, 6:7]:
          e.reciprocal(o, i), waits=[sw])
    p.vsep()
    xw = p.add("vector", lambda e, i=src_ap, o=out_ap, m=stb[:, 2:3],
               rs=stb[:, 6:7]:
               e.tensor_scalar(o, i, m, rs, ALU.subtract, ALU.mult),
               waits=list(extra_write_waits), sig=p.VEC)
    return xw, sqw


def build_program():
    from contextlib import ExitStack
    nc = bass.Bass("TRN2", target_bir_lowering=False)
    st = ExitStack()

    dt = nc.dram_tensor
    xh = dt("x", [SEQS, T, D], F32, kind="ExternalInput")
    yh = dt("y", [SEQS, T, D], F32, kind="ExternalOutput")
    w_kv_h = dt("w_kv", [128, DC, KVR], F32, kind="ExternalInput")
    w_q_h = dt("w_q", [128, DC, QR], F32, kind="ExternalInput")
    w_kr_h = dt("w_kr", [128, DC, QR], F32, kind="ExternalInput")
    w_qr_h = dt("w_qr", [128, 4, QR], F32, kind="ExternalInput")
    w_kd_h = dt("w_kd", [128, 2, D], F32, kind="ExternalInput")
    w_qd_h = dt("w_qd", [128, 4, D], F32, kind="ExternalInput")
    w_vd_h = dt("w_vd", [128, 2, D], F32, kind="ExternalInput")
    w_o_h = dt("w_o", [128, DC, D], F32, kind="ExternalInput")
    wg_h = dt("wg", [128, FFC, D], F32, kind="ExternalInput")
    wu_h = dt("wu", [128, FFC, D], F32, kind="ExternalInput")
    wd_h = dt("wd", [128, DC, FF], F32, kind="ExternalInput")
    cos_h = dt("cos_t", [128, NTT, DR], F32, kind="ExternalInput")
    sinp_h = dt("sinp_t", [128, NTT, DR], F32, kind="ExternalInput")
    tri_h = dt("tri", [128, 128], F32, kind="ExternalInput")
    g_h = dt("g_bc", [128, KVR], F32, kind="ExternalInput")
    cst_h = dt("cst", [128, 8], F32, kind="ExternalInput")

    wgq_s = dt("wgq_s", [D, FF], BF16, kind="Internal")
    wuq_s = dt("wuq_s", [D, FF], BF16, kind="Internal")
    wdq_s = dt("wdq_s", [FF, D], BF16, kind="Internal")
    x2_s = dt("x2_s", [SEQS, T, D], F32, kind="Internal")
    r_s = dt("r_s", [SEQS, H, T], BF16, kind="Internal")
    r_s2 = dt("r_s2", [SEQS, H, T], F32, kind="Internal")

    p = Prog(nc, st)
    sb = lambda shape, d: st.enter_context(nc.sbuf_tensor(_nm(), shape, d))

    def tap(name, src_ap, shape, dtyp):
        if not DEBUG:
            return
        th = nc.dram_tensor("tap_" + name, shape, dtyp, kind="ExternalOutput")
        p.fence()
        p.dma("sync", th[:].rearrange("a b -> (a b)") if len(shape) == 2 else th[:],
              src_ap, p.dsem("TAP" + name))
        p.fence()

    tri = sb([128, 128], BF16)
    g_bc = sb([128, KVR], F32)
    cst = sb([128, 8], F32)
    junk = sb([128, D], F32)
    stt_ = [sb([128, 8], F32) for _ in range(2)]
    psA = st.enter_context(nc.psum_tensor(_nm("psA"), [128, 1024], F32))
    psB = st.enter_context(nc.psum_tensor(_nm("psB"), [128, 1024], F32))
    ps = [st.enter_context(nc.psum_tensor(_nm("ps"), [128, 512], F32)) for _ in range(4)]
    ps = ps + [psA[:, 0:512], psA[:, 512:1024], psB[:, 0:512], psB[:, 512:1024]]
    psS = [psA, psB]

    CONG = p.dsem("CONG")
    CON = p.dsem("CON")
    CONSTS_G = p.dma("gpsimd", tri[:], tri_h[:], CONG)
    p.dma("sync", g_bc[:], g_h[:], CON)
    CONSTS = p.dma("sync", cst[:], cst_h[:], CON)

    with ExitStack() as pro:
        psb = lambda shape, d: pro.enter_context(nc.sbuf_tensor(_nm(), shape, d))
        wtT = psb([128, FFC * D], BF16)
        wtin = [psb([128, FF], F32) for _ in range(2)]
        wtq = [psb([128, FF], BF16) for _ in range(2)]
        qbP = QBufs(pro, nc, 1408)
        WIN = [p.dsem("WIN0"), p.dsem("WIN1")]
        TPS = [p.dsem("TPS0"), p.dsem("TPS1")]
        STS = p.dsem("STS")

        def prep_weight(src_h, ntiles, K, store_ap, dst_fn):
            guards_in = [None, None]
            guards_q = [None, None]
            for nt in range(ntiles):
                r = nt % 2
                w = [CONSTS_G]
                if guards_in[r] is not None:
                    w.append(guards_in[r])
                ld = p.dma("gpsimd", wtin[r][:, 0:K], src_h[:, nt, :], WIN[r],
                           waits=w)
                qw = None
                pw = [ld]
                if guards_q[r] is not None:
                    pw.append(guards_q[r])
                if K <= 1408:
                    qw = quant_unit(p, qbP, wtin[r][:, 0:K], K // 16,
                                    wtq[r][:, 0:K], pre_waits=pw)
                else:
                    for hf in range(K // 1408):
                        qw = quant_unit(p, qbP,
                                        wtin[r][:, hf * 1408:(hf + 1) * 1408],
                                        88, wtq[r][:, hf * 1408:(hf + 1) * 1408],
                                        pre_waits=pw if hf == 0 else ())
                guards_in[r] = qw
                tp = p.dma("sync", dst_fn(nt), wtq[r][:, 0:K], TPS[r],
                           waits=[qw], transpose=True)
                guards_q[r] = tp
            return p.dma("sync", store_ap, wtT[:], STS,
                         waits=[(TPS[0], TPS[0].v), (TPS[1], TPS[1].v)])

        def g_dst(nt):
            return bass.AP(wtT, nt * 128, [[FFC * D, 128], [FF, DC], [1, 128]])

        def d_dst(nt):
            return bass.AP(wtT, nt * 128, [[FFC * D, 128], [D, FFC], [1, 128]])

        gq_ap = bass.AP(wgq_s, 0, [[FF, 128], [128 * FF, DC], [1, FF]])
        uq_ap = bass.AP(wuq_s, 0, [[FF, 128], [128 * FF, DC], [1, FF]])
        dq_ap = bass.AP(wdq_s, 0, [[D, 128], [128 * D, FFC], [1, D]])
        prep_weight(wg_h, FFC, D, gq_ap, g_dst)
        p.fence()
        prep_weight(wu_h, FFC, D, uq_ap, g_dst)
        p.fence()
        prep_weight(wd_h, DC, FF, dq_ap, d_dst)
        p.fence()
        p.flush()

    sems_cache = {}
    for s in range(SEQS):
        with ExitStack() as seqst:
            with ExitStack() as bst:
                bsb = lambda shape, d: bst.enter_context(
                    nc.sbuf_tensor(_nm(), shape, d, side="right"))
                kvn_fm = bsb([128, 2 * T], BF16)
                qc_fm = bsb([128, 4 * T], BF16)
                kr_tok = bsb([128, NTT * 512], BF16)
                qr_tok = bsb([128, NTT * 512], BF16)

                with ExitStack() as cst1:
                    csb = lambda shape, d: cst1.enter_context(
                        nc.sbuf_tensor(_nm(), shape, d, side="right"))
                    xn_fm = csb([128, DC * T], BF16)
                    w_kv = csb([128, DC * KVR], BF16)
                    w_q = csb([128, DC * QR], BF16)
                    w_kr = csb([128, DC * QR], BF16)
                    w_qr = csb([128, 4 * QR], BF16)
                    cos_t = csb([128, NTT * DR], F32)
                    sinp_t = csb([128, NTT * DR], F32)
                    if s == 0:
                        ROPE_T = p.dsem("ROPET")
                        LDX = [p.dsem("LDX0"), p.dsem("LDX1")]
                        TPN = [p.dsem("TPN0"), p.dsem("TPN1")]
                        KVT = [p.dsem("KVT0"), p.dsem("KVT1")]
                        RPT = [p.dsem("RPT0"), p.dsem("RPT1")]
                        WEARLY = p.dsem("WEARLY")
                        WLATE = p.dsem("WLATE")
                        sems_cache[0] = (ROPE_T, LDX, TPN, KVT, RPT, WEARLY, WLATE)
                    (ROPE_T, LDX, TPN, KVT, RPT, WEARLY, WLATE) = sems_cache[0]
                    p.dma("sync", cos_t[:],
                          cos_h[:].rearrange("p a b -> p (a b)"), ROPE_T)
                    rtw = p.dma("sync", sinp_t[:],
                                sinp_h[:].rearrange("p a b -> p (a b)"), ROPE_T)
                    xt = [csb([128, D], F32) for _ in range(2)]
                    xnt = [csb([128, D], BF16) for _ in range(2)]
                    kvt = [csb([128, KVR], BF16) for _ in range(2)]
                    rpt = [csb([128, QR], BF16) for _ in range(2)]
                    rt1 = csb([128, QR], F32)
                    rt2 = csb([128, QR], F32)

                    ew = []
                    for (dsth, srch) in ((w_kv, w_kv_h), (w_q, w_q_h),
                                         (w_kr, w_kr_h), (w_qr, w_qr_h)):
                        ew.append(p.dma("gpsimd", dsth[:],
                                        srch[:].rearrange("p a b -> p (a b)"),
                                        WEARLY, waits=[CONSTS_G]))
                    EARLYW = ew[-1]

                    xt_guard = [None, None]
                    xnt_guard = [None, None]
                    tp_last = None
                    for t in range(NTT):
                        r = t % 2
                        w = [CONSTS] if t < 2 else [xt_guard[r]]
                        ldx = p.dma("sync", xt[r][:],
                                    xh[s, t * 128:(t + 1) * 128, :], LDX[r],
                                    waits=w)
                        eww = [] if xnt_guard[r] is None else [xnt_guard[r]]
                        xw, _ = ln_unit(p, xt[r][:], stt_[r], xnt[r][:],
                                        junk, cst, ldx, extra_write_waits=eww)
                        xt_guard[r] = xw
                        tp_last = p.dma(
                            "sync",
                            bass.AP(xn_fm, t * 128,
                                    [[DC * T, 128], [T, DC], [1, 128]]),
                            xnt[r][:], TPN[t % 2], waits=[xw], transpose=True)
                        xnt_guard[r] = tp_last
                    XN_DONE = (TPN[0], TPN[0].v)
                    XN_DONE2 = (TPN[1], TPN[1].v)

                    kv_guard = [None, None]
                    tp_last = None
                    for t in range(NTT):
                        r = t % 2
                        stb = stt_[r]
                        for c in range(DC):
                            w = [XN_DONE, XN_DONE2, EARLYW] if (c == 0 and t == 0) else []
                            p.add("tensor", lambda e, a=bass.AP(
                                xn_fm, c * T + t * 128, [[DC * T, 128], [1, 128]]),
                                b=bass.AP(w_kv, c * KVR, [[DC * KVR, 128], [1, KVR]]),
                                o=ps[0][:, 0:KVR], s_=(c == 0), sp=(c == DC - 1):
                                e.matmul(o, a, b, start=s_, stop=sp),
                                waits=w, sig=p.TEN if c == DC - 1 else None)
                        mmw = (p.TEN, p.TEN.v)
                        sqw = p.add("scalar", lambda e, i=ps[0][:, 0:KVR],
                                    o=junk[:, 0:KVR], ac=stb[:, 0:1]:
                                    e.activation(o, i, AF.Square, accum_out=ac),
                                    waits=[mmw], sig=p.SCA)
                        rmw = p.add("scalar", lambda e, i=stb[:, 0:1], o=stb[:, 1:2],
                                    b=cst[:, 2:3]:
                                    e.activation(o, i, AF.Sqrt, bias=b,
                                                 scale=1.0 / KVR),
                                    waits=[sqw], sig=p.SCA)
                        p.add("vector", lambda e, i=stb[:, 1:2], o=stb[:, 2:3]:
                              e.reciprocal(o, i), waits=[rmw])
                        p.vsep()
                        w = [] if kv_guard[r] is None else [kv_guard[r]]
                        kvw = p.add("vector", lambda e, i=ps[0][:, 0:KVR],
                                    o=kvt[r][:], rs=stb[:, 2:3], g=g_bc[:]:
                                    e.scalar_tensor_tensor(o, i, rs, g,
                                                           ALU.mult, ALU.mult),
                                    waits=w, sig=p.VEC)
                        p.add("tensor", None, waits=[kvw])
                        tp_last = p.dma(
                            "sync",
                            bass.AP(kvn_fm, t * 128,
                                    [[2 * T, 128], [T, 2], [1, 128]]),
                            kvt[r][:], KVT[t % 2], waits=[kvw], transpose=True)
                        kv_guard[r] = tp_last
                    KVN_DONE = (KVT[0], KVT[0].v)
                    KVN_DONE2 = (KVT[1], KVT[1].v)

                    qc_guard = [None, None]
                    last_ev = None
                    for fc in range(4):
                        for tq in range(4):
                            k = fc * 4 + tq
                            bank = ps[1 + k % 2]
                            for c in range(DC):
                                w = []
                                if c == 0:
                                    if k == 0:
                                        w = [XN_DONE, XN_DONE2]
                                    elif qc_guard[k % 2] is not None:
                                        w = [qc_guard[k % 2]]
                                p.add("tensor", lambda e, a=bass.AP(
                                    w_q, c * QR + fc * 128,
                                    [[DC * QR, 128], [1, 128]]),
                                    b=bass.AP(xn_fm, c * T + tq * 512,
                                              [[DC * T, 128], [1, 512]]),
                                    o=bank[:], s_=(c == 0), sp=(c == DC - 1):
                                    e.matmul(o, a, b, start=s_, stop=sp),
                                    waits=w, sig=p.TEN if c == DC - 1 else None)
                            mmw = (p.TEN, p.TEN.v)
                            last_ev = p.add("vector", lambda e, i=bank[:],
                                            o=bass.AP(qc_fm, fc * T + tq * 512,
                                                      [[4 * T, 128], [1, 512]]):
                                            e.tensor_copy(o, i),
                                            waits=[mmw], sig=p.VEC)
                            qc_guard[k % 2] = last_ev
                    QC_DONE = last_ev

                    def rope_proj(wsb, nk, src, srcn, dst_tok, first_wait):
                        rw = None
                        for t in range(NTT):
                            r = t % 2
                            bank = ps[3 - t % 2]
                            for c in range(nk):
                                w = []
                                if c == 0:
                                    if t == 0:
                                        w = list(first_wait)
                                    else:
                                        w = [(p.VEC, p.VEC.v)]
                                p.add("tensor", lambda e, a=bass.AP(
                                    src, c * T + t * 128,
                                    [[srcn * T, 128], [1, 128]]),
                                    b=bass.AP(wsb, c * QR, [[nk * QR, 128], [1, QR]]),
                                    o=bank[:], s_=(c == 0), sp=(c == nk - 1):
                                    e.matmul(o, a, b, start=s_, stop=sp),
                                    waits=w, sig=p.TEN if c == nk - 1 else None)
                            mmw = (p.TEN, p.TEN.v)
                            cosb = bass.AP(cos_t, t * DR,
                                           [[NTT * DR, 128], [0, H], [1, DR]])
                            sinlo = bass.AP(sinp_t, t * DR,
                                            [[NTT * DR, 128], [0, H], [1, 16]])
                            sinhi = bass.AP(sinp_t, t * DR + 16,
                                            [[NTT * DR, 128], [0, H], [1, 16]])
                            p4 = bank[:].rearrange("p (h u j) -> p h u j",
                                                   h=H, u=2)
                            r4 = rt2[:].rearrange("p (h u j) -> p h u j",
                                                  h=H, u=2)
                            p.add("vector", lambda e,
                                  i=bank[:].rearrange("p (h j) -> p h j", h=H),
                                  cb=cosb,
                                  o=rt1[:].rearrange("p (h j) -> p h j", h=H):
                                  e.tensor_tensor(o, i, cb, ALU.mult), waits=[mmw])
                            p.add("vector", lambda e, i=p4[:, :, 1, :], sb4=sinlo,
                                  o=r4[:, :, 0, :]:
                                  e.tensor_tensor(o, i, sb4, ALU.mult))
                            p.add("vector", lambda e, i=p4[:, :, 0, :], sb4=sinhi,
                                  o=r4[:, :, 1, :]:
                                  e.tensor_tensor(o, i, sb4, ALU.mult))
                            p.vsep()
                            rw = p.add("vector", lambda e, i=rt1[:], j=rt2[:],
                                       o=dst_tok[:, t * 512:(t + 1) * 512]:
                                       e.tensor_tensor(o, i, j, ALU.add),
                                       sig=p.VEC)
                            p.add("tensor", None, waits=[(p.VEC, p.VEC.v)])
                        return rw

                    KR_DONE = rope_proj(w_kr, DC, xn_fm, DC, kr_tok,
                                        [XN_DONE, XN_DONE2, rtw])
                    QR_DONE = rope_proj(w_qr, 4, qc_fm, 4, qr_tok, [QC_DONE])
                    p.fence()
                    if s == 0:
                        tap("xnfm", xn_fm[:], [128, DC * T], BF16)
                        tap("kvnfm", kvn_fm[:], [128, 2 * T], BF16)
                        tap("qcfm", qc_fm[:], [128, 4 * T], BF16)
                        tap("krtok", kr_tok[:], [128, NTT * 512], BF16)
                        tap("qrtok", qr_tok[:], [128, NTT * 512], BF16)
                    p.flush()

                with ExitStack() as dst_:
                    dsb = lambda shape, d: dst_.enter_context(nc.sbuf_tensor(_nm(), shape, d))
                    xq_fm = seqst.enter_context(
                        nc.sbuf_tensor(_nm(), [128, DC * T], BF16))
                    O_fm = dsb([128, DC * T], BF16)
                    r_sb = dsb([128, 4 * T], BF16)

                    with ExitStack() as grp:
                        gsb = lambda shape, d: grp.enter_context(
                            nc.sbuf_tensor(_nm(), shape, d, side="right"))
                        wkd_sl = gsb([128, 2 * 256], BF16)
                        wqd_sl = gsb([128, 4 * 256], BF16)
                        wvd_sl = gsb([128, 2 * 256], BF16)
                        K_fm = gsb([128, 4 * T], BF16)
                        Q_fm = gsb([128, 4 * T], BF16)
                        V_g = gsb([128, NTT * 384], BF16)
                        PT = [gsb([128, 1024], BF16) for _ in range(2)]
                        ktr = [gsb([128, 512], BF16) for _ in range(2)]
                        if s == 0:
                            WDEC = p.dsem("WDEC")
                            KTP = [p.dsem("KTP0"), p.dsem("KTP1")]
                            sems_cache[3] = (WDEC, KTP)
                        (WDEC, KTP) = sems_cache[3]

                        for g in range(NG):
                            gw_ = [(p.TEN, p.TEN.v)]
                            wld = None
                            for (dsl, srch, nk) in ((wkd_sl, w_kd_h, 2),
                                                    (wqd_sl, w_qd_h, 4),
                                                    (wvd_sl, w_vd_h, 2)):
                                for c in range(nk):
                                    wld = p.dma("gpsimd",
                                                dsl[:, c * 256:(c + 1) * 256],
                                                srch[:, c, g * 256:(g + 1) * 256],
                                                WDEC, waits=gw_ + [CONSTS_G])
                                    gw_ = []
                            WSL = wld

                            for (ridx, rope_tok, dsl, nk, dsrc, srcn, dfm) in (
                                    (0, kr_tok, wkd_sl, 2, kvn_fm, 2, K_fm),
                                    (1, qr_tok, wqd_sl, 4, qc_fm, 4, Q_fm)):
                                for t in range(NTT):
                                    r = t % 2
                                    bank = ps[r]
                                    for c in range(nk):
                                        w = []
                                        if c == 0:
                                            w = [(p.VEC, p.VEC.v), WSL]
                                            if t == 0 and ridx == 0 and g == 0:
                                                w += [KVN_DONE, KVN_DONE2]
                                        p.add("tensor", lambda e, a=bass.AP(
                                            dsrc, c * T + t * 128,
                                            [[srcn * T, 128], [1, 128]]),
                                            b=dsl[:, c * 256:(c + 1) * 256],
                                            o=bank[:, 0:256], s_=(c == 0),
                                            sp=(c == nk - 1):
                                            e.matmul(o, a, b, start=s_, stop=sp),
                                            waits=w,
                                            sig=p.TEN if c == nk - 1 else None)
                                    mmw = (p.TEN, p.TEN.v)
                                    w = [mmw, (KTP[r], KTP[r].v)]
                                    p.add("vector", lambda e, i=bank[:, 0:256],
                                          o=bass.AP(ktr[r], 0,
                                                    [[512, 128], [128, 4], [1, 64]]):
                                          e.tensor_copy(o, i), waits=w)
                                    cw = p.add("vector", lambda e, i=bass.AP(
                                        rope_tok, t * 512 + g * 128,
                                        [[NTT * 512, 128], [1, 128]]),
                                        o=bass.AP(ktr[r], 64,
                                                  [[512, 128], [128, 4], [1, 32]]):
                                        e.tensor_copy(o, i), sig=p.VEC)
                                    p.dma("sync", bass.AP(
                                        dfm, t * 128,
                                        [[4 * T, 128], [T, 4], [1, 128]]),
                                        ktr[r][:], KTP[r], waits=[cw],
                                        transpose=True)
                            KQ_DONE = [(KTP[0], KTP[0].v), (KTP[1], KTP[1].v)]

                            if g == 0:
                                for pr in range(2):
                                    p.add("vector", lambda e, o=bass.AP(
                                        V_g, pr * 192 + 64,
                                        [[NTT * 384, 128], [384, NTT], [1, 64]]):
                                        e.memset(o, 0.0))
                                    p.add("vector", lambda e, o=bass.AP(
                                        V_g, pr * 192 + 64 + 32 * pr,
                                        [[NTT * 384, 128], [384, NTT], [1, 1]]):
                                        e.memset(o, 1.0), sig=p.VEC)
                            STRIPS = (p.VEC, p.VEC.v)
                            for t in range(NTT):
                                bank = ps[t % 2]
                                for c in range(2):
                                    w = []
                                    if c == 0:
                                        w = [(p.VEC, p.VEC.v), WSL]
                                        if t == 0:
                                            w += [STRIPS, KVN_DONE, KVN_DONE2]
                                    p.add("tensor", lambda e, a=bass.AP(
                                        kvn_fm, c * T + t * 128,
                                        [[2 * T, 128], [1, 128]]),
                                        b=wvd_sl[:, c * 256:(c + 1) * 256],
                                        o=bank[:, 0:256], s_=(c == 0), sp=(c == 1):
                                        e.matmul(o, a, b, start=s_, stop=sp),
                                        waits=w, sig=p.TEN if c == 1 else None)
                                mmw = (p.TEN, p.TEN.v)
                                p.add("vector", lambda e,
                                      i=bank[:, 0:256].rearrange(
                                          "p (pr sd dh) -> p pr sd dh", pr=2, sd=2),
                                      o=bass.AP(V_g, t * 384,
                                                [[NTT * 384, 128], [192, 2],
                                                 [128, 2], [1, 64]]):
                                      e.tensor_copy(o, i), waits=[mmw], sig=p.VEC)
                            VG_DONE = (p.VEC, p.VEC.v)
                            if DEBUG:
                                p.fence()
                            if s == 0 and g == 0:
                                tap("kfm", K_fm[:], [128, 4 * T], BF16)
                                tap("qfm", Q_fm[:], [128, 4 * T], BF16)
                                tap("vg", V_g[:], [128, NTT * 384], BF16)

                            for hh in range(GH):
                                h = 4 * g + hh
                                vwin = (hh // 2) * 192 + (64 if (hh % 2) else 0)
                                prow = (64 + 32 * (hh // 2)) if hh % 2 == 0 \
                                    else (32 * (hh // 2))
                                for jb in range(4):
                                    ob = ps[2 + jb % 2]
                                    nk = 4 * jb + 4
                                    pv_bank_w = (p.VEC, p.VEC.v)
                                    half_guard = {}
                                    for ii in range(0, 4 * jb, 2):
                                        pb = (ii // 2) % 2
                                        sp_ = psS[pb]
                                        for d in range(2):
                                            i = ii + d
                                            hk = pb * 2 + d
                                            w = []
                                            if hk in half_guard:
                                                w = [half_guard[hk]]
                                            if ii == 0 and d == 0:
                                                w = w + KQ_DONE
                                            p.add("tensor", lambda e, a=bass.AP(
                                                K_fm, hh * T + i * 128,
                                                [[4 * T, 96], [1, 128]]),
                                                b=bass.AP(Q_fm, hh * T + jb * 512,
                                                          [[4 * T, 96], [1, 512]]),
                                                o=sp_[:, d * 512:(d + 1) * 512]:
                                                e.matmul(o, a, b, start=True,
                                                         stop=True),
                                                waits=w,
                                                sig=p.TEN if d == 1 else None)
                                        qkw = (p.TEN, p.TEN.v)
                                        slot = PT[pb]
                                        ew_ = p.add("scalar", lambda e,
                                                    i_=sp_[:, 0:1024],
                                                    o=slot[:, 0:1024],
                                                    b=cst[:, 0:1]:
                                                    e.activation(o, i_, AF.Exp,
                                                                 bias=b,
                                                                 scale=SCALE),
                                                    waits=[qkw], sig=p.SCA)
                                        half_guard[pb * 2] = ew_
                                        half_guard[pb * 2 + 1] = ew_
                                        for d in range(2):
                                            i = ii + d
                                            w = [ew_]
                                            if i == 0:
                                                w += [pv_bank_w, VG_DONE]
                                            p.add("tensor", lambda e, a=bass.AP(
                                                V_g, i * 384 + vwin,
                                                [[NTT * 384, 128], [1, 128]]),
                                                b=slot[:, d * 512:(d + 1) * 512],
                                                o=ob[:], s_=(i == 0):
                                                e.matmul(o, a, b, start=s_,
                                                         stop=False),
                                                waits=w)
                                    for m in range(4):
                                        i = 4 * jb + m
                                        pb = (i // 2) % 2
                                        dhalf = i % 2
                                        hk = pb * 2 + dhalf
                                        sp_h = psS[pb][:, dhalf * 512:
                                                       dhalf * 512 + 512]
                                        lo = 128 * m
                                        w = []
                                        if hk in half_guard:
                                            w = [half_guard[hk]]
                                        if i == 0:
                                            w = w + KQ_DONE
                                        qkw = p.add("tensor", lambda e, a=bass.AP(
                                            K_fm, hh * T + i * 128,
                                            [[4 * T, 96], [1, 128]]),
                                            b=bass.AP(Q_fm,
                                                      hh * T + jb * 512 + lo,
                                                      [[4 * T, 96], [1, 512 - lo]]),
                                            o=sp_h[:, lo:512]:
                                            e.matmul(o, a, b, start=True,
                                                     stop=True),
                                            waits=w, sig=p.TEN)
                                        slot = PT[pb]
                                        off = dhalf * 512
                                        ew_ = p.add("scalar", lambda e,
                                                    i_=sp_h[:, lo:512],
                                                    o=slot[:, off + lo:off + 512],
                                                    b=cst[:, 0:1]:
                                                    e.activation(o, i_, AF.Exp,
                                                                 bias=b,
                                                                 scale=SCALE),
                                                    waits=[qkw], sig=p.SCA)
                                        half_guard[hk] = ew_
                                        dep = p.add("vector", lambda e,
                                                    o=slot[:, off + lo:
                                                           off + lo + 128],
                                                    tr=tri[:]:
                                                    e.tensor_tensor(o, o, tr,
                                                                    ALU.mult),
                                                    waits=[ew_], sig=p.VEC)
                                        half_guard[hk] = dep
                                        w = [dep]
                                        if i == 0:
                                            w += [pv_bank_w, VG_DONE]
                                        p.add("tensor", lambda e, a=bass.AP(
                                            V_g, i * 384 + vwin,
                                            [[NTT * 384, 128], [1, 128]]),
                                            b=slot[:, off + lo:off + 512],
                                            o=ob[:, lo:512],
                                            s_=(i == 0), sp=(i == nk - 1):
                                            e.matmul(o, a, b, start=s_, stop=sp),
                                            waits=w,
                                            sig=p.TEN if i == nk - 1 else None)
                                    pvw = (p.TEN, p.TEN.v)
                                    orow = 0 if h % 2 == 0 else 64
                                    p.add("vector", lambda e,
                                        i=ob[orow:orow + 64, :],
                                        o=bass.AP(O_fm,
                                                  orow * DC * T + (h // 2) * T
                                                  + jb * 512,
                                                  [[DC * T, 64], [1, 512]]):
                                        e.tensor_copy(o, i), waits=[pvw])
                                    p.add("vector", lambda e,
                                        i=ob[prow:prow + 1, :],
                                        o=bass.AP(r_sb,
                                                  prow * 4 * T + g * T + jb * 512,
                                                  [[4 * T, 1], [1, 512]]):
                                        e.tensor_copy(o, i), sig=p.VEC)
                            p.fence()
                        p.flush()

                    with ExitStack() as eph:
                        esb = lambda shape, d: eph.enter_context(nc.sbuf_tensor(_nm(), shape, d))
                        w_o = esb([128, DC * D], BF16)
                        WOL = p.dma("gpsimd", w_o[:],
                                    w_o_h[:].rearrange("p a b -> p (a b)"),
                                    WLATE, waits=[CONSTS_G])
                        R_rep = esb([128, T], F32)
                        xt2 = [esb([128, D], F32) for _ in range(2)]
                        x2t = [esb([128, D], F32) for _ in range(2)]
                        x2n = esb([128, D], F32)
                        xqt = [esb([128, D], BF16) for _ in range(2)]
                        qbE = QBufs(eph, nc, 512)
                        if s == 0:
                            tap("ofm_pre", O_fm[:], [128, DC * T], BF16)
                            tap("rsb", r_sb[:], [128, 4 * T], BF16)
                            RS1 = p.dsem("RS1")
                            RS2 = p.dsem("RS2")
                            RS3 = p.dsem("RS3")
                            RREP = [p.dsem("RREP0"), p.dsem("RREP1")]
                            X2W = [p.dsem("X2W0"), p.dsem("X2W1")]
                            XQT = [p.dsem("XQT0"), p.dsem("XQT1")]
                            sems_cache[1] = (RS1, RS2, RS3, RREP, X2W, XQT)
                        (RS1, RS2, RS3, RREP, X2W, XQT) = sems_cache[1]
                        rcp = esb([128, 256], BF16)
                        rcp2 = esb([128, 256], F32)

                        w2 = None
                        for g_ in range(NG):
                            w2 = p.dma("sync",
                                       bass.AP(r_s, (s * H + 4 * g_) * T,
                                               [[T, 4], [1, T]]),
                                       bass.AP(r_sb, g_ * T,
                                               [[32 * 4 * T, 4], [1, T]]), RS1)
                        ldr = p.dma("sync", rcp[:],
                                    bass.AP(r_s, s * H * T, [[256, 128], [1, 256]]),
                                    RS2, waits=[w2])
                        rcw = p.add("vector", lambda e, i=rcp[:], o=rcp2[:]:
                                    e.reciprocal(o, i), waits=[ldr], sig=p.VEC)
                        str_ = p.dma("sync",
                                     bass.AP(r_s2, s * H * T, [[256, 128], [1, 256]]),
                                     rcp2[:], RS3, waits=[rcw])
                        ng = [None, None]
                        last_norm = None
                        for h in range(H):
                            half = (h % 2) * 64
                            w = [str_]
                            if ng[h % 2] is not None:
                                w.append(ng[h % 2])
                            rep = p.dma("sync",
                                        bass.AP(R_rep, half * T, [[T, 64], [1, T]]),
                                        bass.AP(r_s2,
                                                (s * H + 4 * (h // 4)
                                                 + SLOT[h % 4]) * T,
                                                [[0, 64], [1, T]]),
                                        RREP[h % 2], waits=w)
                            nw = p.add("vector", lambda e, o=bass.AP(
                                O_fm, half * DC * T + (h // 2) * T,
                                [[DC * T, 64], [1, T]]),
                                rp=bass.AP(R_rep, half * T, [[T, 64], [1, T]]):
                                e.tensor_tensor(o, o, rp, ALU.mult),
                                waits=[rep], sig=p.VEC)
                            ng[h % 2] = nw
                            last_norm = nw
                        if s == 0 and DEBUG:
                            p.fence()
                            tap("ofm_post", O_fm[:], [128, DC * T], BF16)

                        xt_guard = [None, None]
                        x2_guard = [None, None]
                        xq_guard = [None, None]
                        ev_tok = [None, None]
                        tp_last = None
                        for t in range(NTT):
                            r = t % 2
                            w = [] if xt_guard[r] is None else [xt_guard[r]]
                            ldx = p.dma("sync", xt2[r][:],
                                        xh[s, t * 128:(t + 1) * 128, :], LDX[r],
                                        waits=w)
                            evs = None
                            for f2 in range(2):
                                bank = ps[f2]
                                for c in range(DC):
                                    w = []
                                    if c == 0:
                                        if ev_tok[f2] is not None:
                                            w = [ev_tok[f2]]
                                        if t == 0 and f2 == 0:
                                            w.append(last_norm)
                                            w.append(WOL)
                                    p.add("tensor", lambda e, a=bass.AP(
                                        O_fm, c * T + t * 128,
                                        [[DC * T, 128], [1, 128]]),
                                        b=bass.AP(w_o, c * D + f2 * 512,
                                                  [[DC * D, 128], [1, 512]]),
                                        o=bank[:], s_=(c == 0), sp=(c == DC - 1):
                                        e.matmul(o, a, b, start=s_, stop=sp),
                                        waits=w,
                                        sig=p.TEN if c == DC - 1 else None)
                                mmw = (p.TEN, p.TEN.v)
                                w = [mmw, ldx]
                                if x2_guard[r] is not None and f2 == 0:
                                    w += x2_guard[r]
                                evs = p.add("vector", lambda e, i=bank[:],
                                            j=xt2[r][:, f2 * 512:(f2 + 1) * 512],
                                            o=x2t[r][:, f2 * 512:(f2 + 1) * 512]:
                                            e.tensor_tensor(o, i, j, ALU.add),
                                            waits=w, sig=p.VEC)
                                ev_tok[f2] = evs
                            xt_guard[r] = evs
                            x2w = p.dma("sync", x2_s[s, t * 128:(t + 1) * 128, :],
                                        x2t[r][:], X2W[r], waits=[evs])
                            lw_, sqw = ln_unit(p, x2t[r][:], stt_[r], x2n[:],
                                               junk, cst, evs)
                            x2_guard[r] = [lw_, x2w, sqw]
                            qw = None
                            for hf in range(2):
                                pw = [lw_]
                                if hf == 0 and xq_guard[r] is not None:
                                    pw.append(xq_guard[r])
                                qw = quant_unit(p, qbE,
                                                x2n[:, hf * 512:(hf + 1) * 512],
                                                32, xqt[r][:, hf * 512:(hf + 1) * 512],
                                                pre_waits=pw)
                            tp_last = p.dma(
                                "sync",
                                bass.AP(xq_fm, t * 128,
                                        [[DC * T, 128], [T, DC], [1, 128]]),
                                xqt[r][:], XQT[t % 2], waits=[qw], transpose=True)
                            xq_guard[r] = tp_last
                        XQ_DONE = (XQT[0], XQT[0].v)
                        XQ_DONE2 = (XQT[1], XQT[1].v)
                        p.fence()
                        if s == 0:
                            tap("xqfm", xq_fm[:], [128, DC * T], BF16)
                        p.flush()

            with ExitStack() as ffn:
                fsb = lambda shape, d: ffn.enter_context(nc.sbuf_tensor(_nm(), shape, d))
                wgq = fsb([128, DC * FF], BF16)
                wuq = fsb([128, DC * FF], BF16)
                h_tok = fsb([128, 4 * FF], BF16)
                hqT = fsb([128, FFC * 512], BF16)
                g_sb = [fsb([128, 512], BF16) for _ in range(2)]
                wdq = [fsb([128, D], BF16) for _ in range(4)]
                x2r = [fsb([128, D], F32) for _ in range(2)]
                yob = [fsb([128, 512], F32) for _ in range(2)]
                qbF = QBufs(ffn, nc, 512)
                if s == 0:
                    WGL = p.dsem("WGL")
                    WDQ = [p.dsem(f"WDQ{i}") for i in range(4)]
                    HQT = p.dsem("HQT")
                    YST = [p.dsem("YST0"), p.dsem("YST1")]
                    sems_cache[2] = (WGL, WDQ, HQT, YST)
                (WGL, WDQ, HQT, YST) = sems_cache[2]

                ldg = p.dma("sync", wgq[:],
                            bass.AP(wgq_s, 0, [[FF, 128], [128 * FF, DC], [1, FF]]),
                            WGL)
                ldu = p.dma("sync", wuq[:],
                            bass.AP(wuq_s, 0, [[FF, 128], [128 * FF, DC], [1, FF]]),
                            WGL)
                NS = [512] * 5 + [256]
                x2r_guard = [None, None]
                wd_guard = [None] * 4
                prev_down = None
                sca_tok = [None, None]
                vec_tok = [None, None]
                yw_tok = [None] * 8
                for ts4 in range(4):
                    for nsi in range(6):
                        nw_ = NS[nsi]
                        n0 = nsi * 512
                        for tt in range(4):
                            t = ts4 * 4 + tt
                            gb = ps[tt % 2]
                            ub = ps[2 + tt % 2]
                            for c in range(DC):
                                w = []
                                if c == 0:
                                    w = [x for x in (sca_tok[tt % 2],
                                                     yw_tok[tt % 2])
                                         if x is not None]
                                    if ts4 == 0 and nsi == 0 and tt == 0:
                                        w += [XQ_DONE, XQ_DONE2, ldg, ldu]
                                p.add("tensor", lambda e, a=bass.AP(
                                    xq_fm, c * T + t * 128,
                                    [[DC * T, 128], [1, 128]]),
                                    b=bass.AP(wgq, c * FF + n0,
                                              [[DC * FF, 128], [1, nw_]]),
                                    o=gb[:, 0:nw_], s_=(c == 0), sp=(c == DC - 1):
                                    e.matmul(o, a, b, start=s_, stop=sp),
                                    waits=w, sig=p.TEN if c == DC - 1 else None)
                            gmw = (p.TEN, p.TEN.v)
                            for c in range(DC):
                                w = []
                                if c == 0:
                                    w = [x for x in (vec_tok[tt % 2],
                                                     yw_tok[2 + tt % 2])
                                         if x is not None]
                                p.add("tensor", lambda e, a=bass.AP(
                                    xq_fm, c * T + t * 128,
                                    [[DC * T, 128], [1, 128]]),
                                    b=bass.AP(wuq, c * FF + n0,
                                              [[DC * FF, 128], [1, nw_]]),
                                    o=ub[:, 0:nw_], s_=(c == 0), sp=(c == DC - 1):
                                    e.matmul(o, a, b, start=s_, stop=sp),
                                    waits=w, sig=p.TEN if c == DC - 1 else None)
                            umw = (p.TEN, p.TEN.v)
                            sl_w = [gmw]
                            if vec_tok[tt % 2] is not None:
                                sl_w.append(vec_tok[tt % 2])
                            sw = p.add("scalar", lambda e, i=gb[:, 0:nw_],
                                       o=g_sb[tt % 2][:, 0:nw_]:
                                       e.activation(o, i, AF.Silu),
                                       waits=sl_w, sig=p.SCA)
                            sca_tok[tt % 2] = sw
                            hm_w = [sw, umw]
                            if nsi == 0:
                                hm_w.append((HQT, HQT.v))
                            vec_tok[tt % 2] = p.add(
                                "vector", lambda e, i=g_sb[tt % 2][:, 0:nw_],
                                j=ub[:, 0:nw_],
                                o=h_tok[:, tt * FF + n0: tt * FF + n0 + nw_]:
                                e.tensor_tensor(o, i, j, ALU.mult),
                                waits=hm_w, sig=p.VEC)
                    H_DONE = (p.VEC, p.VEC.v)
                    hq_last = None
                    QCH = [512] * 5 + [256]
                    for tt in range(4):
                        qw = None
                        q0 = 0
                        for qt in range(6):
                            pw = []
                            if tt == 0 and qt == 0:
                                pw = [H_DONE]
                            qw = quant_unit(
                                p, qbF,
                                h_tok[:, tt * FF + q0: tt * FF + q0 + QCH[qt]],
                                QCH[qt] // 16,
                                h_tok[:, tt * FF + q0: tt * FF + q0 + QCH[qt]],
                                pre_waits=pw)
                            q0 += QCH[qt]
                        tq_w = [qw]
                        if prev_down is not None:
                            tq_w.append(prev_down)
                        hq_last = p.dma(
                            "sync",
                            bass.AP(hqT, tt * 128,
                                    [[FFC * 512, 128], [512, FFC], [1, 128]]),
                            h_tok[:, tt * FF: tt * FF + FF], HQT,
                            waits=tq_w, transpose=True)
                    HQ_DONE = hq_last
                    for nci in range(FFC):
                        r4 = nci % 4
                        w = []
                        if wd_guard[r4] is not None:
                            w = [wd_guard[r4]]
                        ldw = p.dma("sync", wdq[r4][:],
                                    wdq_s[nci * 128:(nci + 1) * 128, :], WDQ[r4],
                                    waits=w)
                        for tt in range(4):
                            for f2 in range(2):
                                bank = ps[tt * 2 + f2]
                                w = []
                                if tt == 0 and f2 == 0:
                                    w = [ldw]
                                    if nci == 0:
                                        w += [HQ_DONE, (p.VEC, p.VEC.v)]
                                p.add("tensor", lambda e, a=bass.AP(
                                    hqT, nci * 512 + tt * 128,
                                    [[FFC * 512, 128], [1, 128]]),
                                    b=wdq[r4][:, f2 * 512:(f2 + 1) * 512],
                                    o=bank[:], s_=(nci == 0), sp=(nci == FFC - 1):
                                    e.matmul(o, a, b, start=s_, stop=sp),
                                    waits=w,
                                    sig=p.TEN if (tt == 3 and f2 == 1) else None)
                        wd_guard[r4] = (p.TEN, p.TEN.v)
                    DOWN_DONE = (p.TEN, p.TEN.v)
                    prev_down = DOWN_DONE
                    yst = None
                    for tt in range(4):
                        t = ts4 * 4 + tt
                        r = tt % 2
                        w = [] if x2r_guard[r] is None else [x2r_guard[r]]
                        ldx2 = p.dma("sync", x2r[r][:],
                                     x2_s[s, t * 128:(t + 1) * 128, :], LDX[r],
                                     waits=w)
                        for f2 in range(2):
                            bank = ps[tt * 2 + f2]
                            yw = p.add("vector", lambda e, i=bank[:],
                                       j=x2r[r][:, f2 * 512:(f2 + 1) * 512],
                                       o=yob[f2][:]:
                                       e.tensor_tensor(o, i, j, ALU.add),
                                       waits=[DOWN_DONE, ldx2,
                                              (YST[f2], YST[f2].v)], sig=p.VEC)
                            x2r_guard[r] = yw
                            yw_tok[tt * 2 + f2] = yw
                            yst = p.dma(
                                "sync",
                                yh[s, t * 128:(t + 1) * 128,
                                   f2 * 512:(f2 + 1) * 512],
                                yob[f2][:], YST[f2], waits=[yw])
                    if ts4 == 3:
                        p.fence()
                p.flush()

    p.ops["sync"].append(([(s_.h, s_.v) for s_ in p.dma_sems], None, [], False))
    p.flush()
    st.close()
    return nc


def _rope_tables():
    inv = 1.0 / 10000.0 ** (np.arange(0, DR, 2, dtype=np.float32) / DR)
    t = np.arange(T, dtype=np.float32)
    freqs = np.outer(t, inv)
    emb = np.concatenate([freqs, freqs], -1)
    cos = np.cos(emb).astype(np.float32)
    sinp = np.sin(emb).astype(np.float32)
    sinp[:, :DR // 2] *= -1.0
    cos_t = np.ascontiguousarray(cos.reshape(NTT, 128, DR).transpose(1, 0, 2))
    sinp_t = np.ascontiguousarray(sinp.reshape(NTT, 128, DR).transpose(1, 0, 2))
    return cos_t, sinp_t


def _rope_perm():
    perm = np.zeros(QR, dtype=np.int64)
    for rc in range(4):
        for m in range(4):
            h = 4 * rc + m
            sl = SLOT[m]
            perm[rc * 128 + sl * 32: rc * 128 + sl * 32 + 32] = \
                np.arange(h * 32, h * 32 + 32)
    return perm


def _pack_km(w):
    K, F = w.shape
    return np.ascontiguousarray(w.reshape(K // 128, 128, F).transpose(1, 0, 2))


def _pack_nk(w):
    N, K = w.shape
    return np.ascontiguousarray(w.reshape(N // 128, 128, K).transpose(1, 0, 2))


_CACHE = {}


def kernel(L0_attn_x, L0_attn_W_kv, L0_attn_g_kv, L0_attn_W_kdec, L0_attn_W_vdec,
           L0_attn_W_q, L0_attn_W_qdec, L0_attn_W_krope, L0_attn_W_qrope,
           L0_attn_W_o, L0_ffn_w_gate, L0_ffn_w_up, L0_ffn_w_down):
    x = np.ascontiguousarray(np.asarray(L0_attn_x, dtype=np.float32))
    perm = _rope_perm()
    cos_t, sinp_t = _rope_tables()
    tri = np.triu(np.ones((128, 128), dtype=np.float32))
    g_bc = np.ascontiguousarray(np.broadcast_to(
        np.asarray(L0_attn_g_kv, np.float32)[None, :], (128, KVR)))
    cst = np.zeros((128, 8), dtype=np.float32)
    cst[:, 0] = CSHIFT
    cst[:, 1] = 1e-5
    cst[:, 2] = 1e-6

    wmaps = {
        "w_kv": _pack_km(np.asarray(L0_attn_W_kv, np.float32)),
        "w_q": _pack_km(np.asarray(L0_attn_W_q, np.float32)),
        "w_kr": _pack_km(np.asarray(L0_attn_W_krope, np.float32)),
        "w_qr": _pack_km(np.asarray(L0_attn_W_qrope, np.float32)),
        "w_kd": _pack_km(np.asarray(L0_attn_W_kdec, np.float32)),
        "w_qd": _pack_km(np.asarray(L0_attn_W_qdec, np.float32)),
        "w_vd": _pack_km(np.asarray(L0_attn_W_vdec, np.float32)),
        "w_o": _pack_km(np.asarray(L0_attn_W_o, np.float32)),
        "wg": _pack_nk(np.asarray(L0_ffn_w_gate, np.float32)),
        "wu": _pack_nk(np.asarray(L0_ffn_w_up, np.float32)),
        "wd": _pack_nk(np.asarray(L0_ffn_w_down, np.float32)),
        "cos_t": cos_t, "sinp_t": sinp_t, "tri": tri, "g_bc": g_bc, "cst": cst,
    }

    if "nc" not in _CACHE:
        _CACHE["nc"] = build_program()
    nc = _CACHE["nc"]

    in_maps = []
    for c in range(NCORES):
        m = dict(wmaps)
        m["x"] = np.ascontiguousarray(x[2 * c: 2 * c + 2])
        in_maps.append(m)
    res = run_bass_kernel_spmd(nc, in_maps, core_ids=list(range(NCORES)))
    out = np.concatenate([np.asarray(res.results[c]["y"]) for c in range(NCORES)],
                         axis=0)
    return out.astype(np.float32)
